# version 11
# baseline (speedup 1.0000x reference)
from contextlib import ExitStack

import numpy as np
import ml_dtypes

import concourse.bass as bass
import concourse.mybir as mybir
from concourse.bass_utils import run_bass_kernel_spmd

F32 = mybir.dt.float32
BF16 = mybir.dt.bfloat16
I32 = mybir.dt.int32
AF = mybir.ActivationFunctionType
ALU = mybir.AluOpType
AX = mybir.AxisListType.X

MAGIC = float(np.float32(1.5 * 2 ** 23))
QMAX = 127.0

N_CORES = 8
T_CORE = 4096
D = 1024
I_FFN = 2816
NH = 16
HD = 64
WIN = 256
N_TILES = T_CORE // 128
N_IC = I_FFN // 128
N_CC = D // 128
FFN_ST = 1024
N_FFN_ST = T_CORE // FFN_ST
P5_ST = 256
N_P5_ST = T_CORE // P5_ST
ATT_ST = 512
N_ATT_ST = T_CORE // ATT_ST
EPS = 1e-5

ALL_SEMS = ("pe", "ve", "se", "po", "sx", "sw", "sg", "sq", "cc")


class Sched:

    ENGINES = ("tensor", "vector", "scalar", "gpsimd", "sync")

    def __init__(self, nc):
        self.nc = nc
        self.ops = {e: [] for e in self.ENGINES}
        self.es = ExitStack()
        self.sems = {}
        self.counts = {}
        self.waited = {}
        self.ps_w = [0] * 8
        self.ps_r = [dict() for _ in range(8)]
        self.pending_small = {}
        self.eng_sem = {"tensor": "pe", "vector": "ve", "scalar": "se",
                        "gpsimd": "po"}

    def sem(self, name):
        if name not in self.sems:
            self.sems[name] = self.es.enter_context(self.nc.semaphore(name))
            self.counts[name] = 0
        return self.sems[name]

    def emit(self, engine, fn, waits=(), inc=None, inc_amt=1, small=False):
        if inc:
            self.sem(inc)
        ws = []
        if engine in ("vector", "scalar", "gpsimd") and \
                inc == self.eng_sem.get(engine) and self.counts.get(inc, 0) > 0:
            ws.append((inc, self.counts[inc]))
        ps = self.pending_small.pop(engine, None)
        if ps is not None:
            ws.append(ps)
            self.sem(ps[0])
        for s, v in waits:
            self.sem(s)
            assert v <= self.counts[s], f"wait on future token {s}:{v}"
            if v > 0:
                ws.append((s, v))
        tok = None
        if inc is not None:
            self.counts[inc] += inc_amt
            tok = self.counts[inc]
        if small:
            assert inc == self.eng_sem.get(engine), \
                f"small op on {engine} must inc {self.eng_sem.get(engine)}"
            self.pending_small[engine] = (inc, tok)
        self.ops[engine].append((fn, tuple(ws), inc, inc_amt))
        return tok

    def val(self, semname):
        return self.counts.get(semname, 0)

    def dma(self, queue, out, in_, sem, waits=()):
        return self.emit(queue, lambda e, o=out, i=in_: e.dma_start(out=o, in_=i),
                         waits=waits, inc=sem, inc_amt=16)

    def ps_write_waits(self, banks):
        ws = []
        for b in banks:
            for s, v in self.ps_r[b].items():
                ws.append((s, v))
        return ws

    def ps_mark_write(self, banks, pe_tok):
        for b in banks:
            self.ps_w[b] = pe_tok
            self.ps_r[b] = {}

    def ps_read_waits(self, banks):
        return [("pe", max(self.ps_w[b] for b in banks))]

    def ps_mark_read(self, banks, sem, tok):
        for b in banks:
            self.ps_r[b][sem] = max(self.ps_r[b].get(sem, 0), tok)

    COMPUTE_SEMS = ("pe", "ve", "se", "po")

    def _coalesce_incs(self):
        waited_vals = {}
        for e in self.ENGINES:
            for _, waits, _, _ in self.ops[e]:
                for s, v in waits:
                    waited_vals.setdefault(s, set()).add(v)
        for e in self.ENGINES:
            newops = []
            running = {}
            pending = {}
            last_inc_idx = {}
            for fn, waits, inc, inc_amt in self.ops[e]:
                newops.append([fn, waits, None, 0])
                if inc is None or inc not in self.COMPUTE_SEMS:
                    if inc is not None:
                        newops[-1][2] = inc
                        newops[-1][3] = inc_amt
                    continue
                running[inc] = running.get(inc, 0) + inc_amt
                pending[inc] = pending.get(inc, 0) + inc_amt
                last_inc_idx[inc] = len(newops) - 1
                if running[inc] in waited_vals.get(inc, ()):
                    newops[-1][2] = inc
                    newops[-1][3] = pending[inc]
                    pending[inc] = 0
            for s, amt in pending.items():
                if amt:
                    idx = last_inc_idx[s]
                    if newops[idx][2] is None:
                        newops[idx][2] = s
                        newops[idx][3] = amt
                    else:
                        assert newops[idx][2] == s
                        newops[idx][3] += amt
            self.ops[e] = [tuple(t) for t in newops]

    def replay(self, engine, eng):
        for fn, waits, inc, inc_amt in self.ops[engine]:
            for s, v in waits:
                key = (engine, s)
                if v <= self.waited.get(key, 0):
                    continue
                eng.wait_ge(self.sems[s], v)
                self.waited[key] = v
            inst = fn(eng)
            if inc is not None and inc_amt > 0:
                assert inst is not None
                inst.then_inc(self.sems[inc], inc_amt)

    def build(self):
        self._coalesce_incs()
        with self.nc.Block() as blk:
            blk.tensor(lambda eng: self.replay("tensor", eng))
            blk.vector(lambda eng: self.replay("vector", eng))
            blk.scalar(lambda eng: self.replay("scalar", eng))
            blk.gpsimd(lambda eng: self.replay("gpsimd", eng))
            blk.sync(lambda eng: self.replay("sync", eng))
        self.es.close()


def build_program(dbg=(), stop_after=None, p3_st=None, p3_ic=None, att_st=None, att_rounds=True, att_lvl=3):
    dbg = set(dbg)
    nc = bass.Bass("TRN2", target_bir_lowering=False)

    x_in = nc.dram_tensor("x", [T_CORE, D], F32, kind="ExternalInput").ap()
    w_attn = nc.dram_tensor("w_attn", [3 * D, D], F32, kind="ExternalInput").ap()
    w_proj = nc.dram_tensor("w_proj", [D, D], F32, kind="ExternalInput").ap()
    w_gate = nc.dram_tensor("w_gate", [I_FFN, D], F32, kind="ExternalInput").ap()
    w_up = nc.dram_tensor("w_up", [I_FFN, D], F32, kind="ExternalInput").ap()
    w_down = nc.dram_tensor("w_down", [D, I_FFN], F32, kind="ExternalInput").ap()
    clip_x = nc.dram_tensor("clip_x", [1, 1], F32, kind="ExternalInput").ap()
    clip_down = nc.dram_tensor("clip_down", [1, 1], F32, kind="ExternalInput").ap()
    eyef_in = nc.dram_tensor("eye_f32", [128, 128], F32, kind="ExternalInput").ap()
    out = nc.dram_tensor("out", [T_CORE, D], F32, kind="ExternalOutput").ap()

    wgu_s = nc.dram_tensor("wgu_s", [128, N_IC * 16 * 128], BF16, kind="Internal").ap()
    wd_s = nc.dram_tensor("wd_s", [128, N_IC * D], BF16, kind="Internal").ap()
    wqk_s = nc.dram_tensor("wqk_s", [128, 16 * D], BF16, kind="Internal").ap()
    wv_s = nc.dram_tensor("wv_s", [128, N_CC * D], BF16, kind="Internal").ap()
    wp_s = nc.dram_tensor("wp_s", [128, N_CC * D], BF16, kind="Internal").ap()
    h_s = nc.dram_tensor("h_s", [128, N_IC * T_CORE], BF16, kind="Internal").ap()
    cc_in = nc.dram_tensor("cc_in", [1, 1], F32, kind="Internal").ap()
    cc_out = nc.dram_tensor("cc_out", [1, 1], F32, kind="Internal").ap()

    sch = Sched(nc)
    es = sch.es

    def sb(name, shape, dtype):
        return nc.alloc_sbuf_tensor(name, shape, dtype)

    XRES = sb("XRES", [128, N_TILES * 1024], F32)
    EYEB = sb("EYEB", [128, 128], BF16)
    EYEF = sb("EYEF", [128, 128], F32)
    ONES = sb("ONES", [1, 128], F32)
    SUM = sb("SUM", [128, N_TILES], F32)
    SSQ = sb("SSQ", [128, N_TILES], F32)
    MX = sb("MX", [128, N_TILES], F32)
    MN = sb("MN", [128, N_TILES], F32)
    MEAN = sb("MEAN", [128, N_TILES], F32)
    RSTD = sb("RSTD", [128, N_TILES], F32)
    QA = sb("QA", [128, N_TILES], F32)
    QB = sb("QB", [128, N_TILES], F32)
    T0 = sb("T0", [128, N_TILES], F32)
    T1 = sb("T1", [128, N_TILES], F32)
    RMAX = sb("RMAX", [128, 1], F32)
    HMC = sb("HMC", [128, 1], F32)
    SCL = sb("SCL", [128, 8], F32)
    SCL1 = sb("SCL1", [1, 8], F32)
    SGATE = sb("SGATE", [128, N_IC], F32)
    SUP = sb("SUP", [128, N_IC], F32)
    RSG = sb("RSG", [128, N_IC], F32)
    RSU = sb("RSU", [128, N_IC], F32)
    SIGX = sb("SIGX", [1, 2], F32)
    GMAXS = sb("GMAXS", [1, 2], F32)
    NEGM = sb("NEGM", [128, 1], F32)
    SUMS = sb("SUMS", [128, 8], F32)
    RECS = sb("RECS", [128, 8], F32)
    QTMP = sb("QTMP", [128, 1024], F32)

    ARENA_BYTES = 74240
    ARENA = nc.alloc_sbuf_tensor("ARENA", [128, ARENA_BYTES], mybir.dt.uint8)
    abase = nc.lookup_mloc(ARENA).addr

    def at(name, shape, dtype, off):
        nbytes = int(np.prod(shape[1:])) * mybir.dt.size(dtype)
        assert off + nbytes <= ARENA_BYTES, f"{name} overflows arena"
        return nc.alloc_sbuf_tensor_at(name, shape, dtype, offset=abase + off)

    PREPW = at("PREPW", [128, 2 * 2816], F32, 0)
    PREPS = at("PREPS", [128, 2816], F32, 22528)
    PREPB = at("PREPB", [128, 2 * 2816], BF16, 33792)
    PREPO = at("PREPO", [128, 2 * 2816], BF16, 45056)
    PREPR = at("PREPR", [128, 1], F32, 56320)
    P1X = at("P1X", [128, 2 * 1024], F32, 59392)
    XTILE = at("XTILE", [128, 2 * 1024], F32, 0)
    XQTM = at("XQTM", [128, 1024], BF16, 8192)
    XINT = at("XINT", [128, 2 * N_CC * FFN_ST], BF16, 10240)
    WGUB = at("WGUB", [128, 2 * 2048], BF16, 43008)
    GEV = at("GEV", [128, 2 * 1024], F32, 51200)
    HEV = at("HEV", [128, 2 * 1024], F32, 59392)
    HSP = at("HSP", [128, 2 * 1024], BF16, 67584)
    WDT = at("WDT", [128, N_IC * D], BF16, 0)
    HRAW = at("HRAW", [128, 2 * 2 * P5_ST], BF16, 45056)
    HQ = at("HQ", [128, 2 * N_IC * P5_ST], BF16, 47104)
    P5S = at("P5S", [128, 2 * P5_ST], F32, 69632)
    QKS = at("QKS", [128, 16 * ATT_ST], BF16, 0)
    VS = at("VS", [128, (ATT_ST // 128) * D], BF16, 16384)
    XLNF = at("XLNF", [128, 2 * N_CC * ATT_ST], BF16, 24576)
    XLN = at("XLN", [128, 1024], BF16, 40960)
    WQKB = at("WQKB", [128, 2 * 1024], BF16, 43008)
    WVB = at("WVB", [128, 2 * 1024], BF16, 47104)
    WPB = WVB
    EXPS = at("EXPS", [128, 2 * 8 * WIN], BF16, 51200)
    ATTT = at("ATTT", [128, 8 * WIN], BF16, 59392)
    YSB = at("YSB", [128, (NH // 2) * ATT_ST], BF16, 63488)

    PSH = nc.alloc_psum_tensor("PS", [128, 4096], F32)
    PS = PSH[:]
    PSB = PSH.bitcast(BF16)[:]
    PSI = PSH.bitcast(I32)[:]
    ATTT_I = ATTT.bitcast(I32)

    dbg_outs = {}

    def barrier_waits():
        return [(s, sch.val(s)) for s in list(sch.counts.keys())]

    def phase_barrier():
        w = barrier_waits()
        for eng in Sched.ENGINES:
            sch.emit(eng, lambda e: e.nop(), waits=w)

    def dump(name, src, shape, dtype=F32):
        if name not in dbg:
            return
        d = nc.dram_tensor("dbg_" + name, list(shape), dtype,
                           kind="ExternalOutput").ap()
        dbg_outs[name] = d
        sch.dma("sync", d, src, "so", waits=barrier_waits())

    def mm(out_ap, lhsT, rhs, banks, start, stop, waits=(), tile_position=None):
        ws = list(waits) + (sch.ps_write_waits(banks) if start else [])
        tok = sch.emit(
            "tensor",
            lambda e, o=out_ap, l=lhsT, r=rhs, s0=start, s1=stop, tp=tile_position:
                e.matmul(o, l, r, start=s0, stop=s1, tile_position=tp),
            waits=ws, inc="pe")
        sch.ps_mark_write(banks, tok)
        return tok

    def transpose(out_ap, in_ap, banks, waits=(), identity=None):
        idn = identity if identity is not None else EYEB[:]
        ws = list(waits) + sch.ps_write_waits(banks)
        tok = sch.emit(
            "tensor",
            lambda e, o=out_ap, i=in_ap, idn=idn: e.transpose(o, i, idn),
            waits=ws, inc="pe")
        sch.ps_mark_write(banks, tok)
        return tok

    def ps_read(engine, fn, banks, waits=(), small=False):
        sem = {"scalar": "se", "vector": "ve", "gpsimd": "po"}[engine]
        ws = list(waits) + sch.ps_read_waits(banks)
        tok = sch.emit(engine, fn, waits=ws, inc=sem, small=small)
        sch.ps_mark_read(banks, sem, tok)
        return tok

    sch.dma("gpsimd", EYEB[:], eyef_in, "sgi")
    sch.dma("sync", EYEF[:], eyef_in, "si")
    sch.dma("sync", SIGX[0:1, 0:1], clip_x, "si")
    sch.dma("sync", SIGX[0:1, 1:2], clip_down, "si")
    t_eye_g = sch.val("sgi")
    t_eye = sch.val("si")
    sch.emit("vector", lambda e: e.memset(ONES[:], 1.0), inc="ve")
    t = sch.val("si")
    sch.emit("scalar", lambda e: e.activation(out=SCL1[0:1, 0:2],
                                              in_=SIGX[0:1, 0:2],
                                              func=AF.Exp, scale=np.float32(-1.0)),
             waits=[("si", t)], inc="se", small=True)
    sch.emit("vector", lambda e: e.tensor_scalar(out=SCL1[0:1, 0:2],
                                                 in0=SCL1[0:1, 0:2], scalar1=1.0,
                                                 scalar2=None, op0=ALU.add),
             waits=[("se", sch.val("se"))], inc="ve", small=True)
    t_sig = sch.emit("vector", lambda e: e.reciprocal(out=SIGX[0:1, 0:2],
                                                      in_=SCL1[0:1, 0:2]),
                     inc="ve", small=True)

    def ln_stats(tile_src_fn, with_max):
        for t in range(N_TILES):
            src, waits = tile_src_fn(t)
            sch.emit("scalar",
                     lambda e, s=src, t=t: e.activation(
                         out=QTMP[:], in_=s, func=AF.Square,
                         accum_out=SSQ[:, t:t + 1]),
                     waits=waits, inc="se", small=True)
            sch.emit("vector",
                     lambda e, s=src, t=t: e.tensor_reduce(
                         out=SUM[:, t:t + 1], in_=s, axis=AX, op=ALU.add),
                     waits=waits, inc="ve", small=True)
            if with_max:
                sch.emit("vector",
                         lambda e, s=src, t=t: e.tensor_reduce(
                             out=MX[:, t:t + 1], in_=s, axis=AX, op=ALU.max),
                         waits=waits, inc="ve", small=True)
                sch.emit("vector",
                         lambda e, s=src, t=t: e.tensor_reduce(
                             out=MN[:, t:t + 1], in_=s, axis=AX, op=ALU.min),
                         waits=waits, inc="ve", small=True)

    def ln_finalize(with_max):
        sl = slice(0, N_TILES)
        w = [("se", sch.val("se")), ("ve", sch.val("ve")), ("po", sch.val("po"))]
        sch.emit("vector", lambda e: e.tensor_scalar(
            out=MEAN[:, sl], in0=SUM[:, sl], scalar1=1.0 / D, scalar2=None,
            op0=ALU.mult), waits=w, inc="ve", small=True)
        sch.emit("vector", lambda e: e.tensor_scalar(
            out=T0[:, sl], in0=SSQ[:, sl], scalar1=1.0 / D, scalar2=None,
            op0=ALU.mult), inc="ve", small=True)
        sch.emit("vector", lambda e: e.tensor_tensor(
            out=T1[:, sl], in0=MEAN[:, sl], in1=MEAN[:, sl], op=ALU.mult),
            inc="ve", small=True)
        tvar = sch.emit("vector", lambda e: e.tensor_tensor(
            out=T0[:, sl], in0=T0[:, sl], in1=T1[:, sl], op=ALU.subtract),
            inc="ve", small=True)
        sch.emit("scalar", lambda e: e.activation(
            out=T1[:, sl], in_=T0[:, sl], func=AF.Ln, bias=np.float32(EPS)),
            waits=[("ve", tvar)], inc="se", small=True)
        trstd = sch.emit("scalar", lambda e: e.activation(
            out=RSTD[:, sl], in_=T1[:, sl], func=AF.Exp, scale=np.float32(-0.5)),
            inc="se", small=True)
        if not with_max:
            return trstd
        sch.emit("vector", lambda e: e.tensor_tensor(
            out=T0[:, sl], in0=MX[:, sl], in1=MEAN[:, sl], op=ALU.subtract),
            waits=[("se", trstd)], inc="ve", small=True)
        sch.emit("vector", lambda e: e.tensor_tensor(
            out=T1[:, sl], in0=MEAN[:, sl], in1=MN[:, sl], op=ALU.subtract),
            inc="ve", small=True)
        sch.emit("vector", lambda e: e.tensor_tensor(
            out=T0[:, sl], in0=T0[:, sl], in1=T1[:, sl], op=ALU.max),
            inc="ve", small=True)
        sch.emit("vector", lambda e: e.tensor_tensor(
            out=T0[:, sl], in0=T0[:, sl], in1=RSTD[:, sl], op=ALU.mult),
            inc="ve", small=True)
        tred = sch.emit("vector", lambda e: e.tensor_reduce(
            out=RMAX[:], in_=T0[:, sl], axis=AX, op=ALU.max),
            inc="ve", small=True)
        return tred

    def partition_max_to_scalar(src_col, tred):
        transpose(PS[0:1, 3584:3712], src_col, banks=[7],
                  waits=[("ve", tred), ("si", t_eye)], identity=EYEF[:])
        tgm = ps_read("vector", lambda e: e.tensor_reduce(
            out=T0[0:1, 0:1], in_=PS[0:1, 3584:3712], axis=AX, op=ALU.max),
            banks=[7], small=True)
        return tgm

    def global_max(idx, tgm):
        ts = sch.dma("gpsimd", cc_in, T0[0:1, 0:1], "sgc",
                     waits=[("ve", tgm), ("cc", sch.val("cc"))])
        tcc = sch.emit("gpsimd",
                       lambda e: e.collective_compute(
                           "AllReduce", ALU.max,
                           replica_groups=[list(range(N_CORES))],
                           ins=[cc_in.opt()], outs=[cc_out.opt()]),
                       waits=[("sgc", ts)], inc="cc")
        tb = sch.dma("gpsimd", GMAXS[0:1, idx:idx + 1], cc_out, "sgc",
                     waits=[("cc", tcc)])
        return tb

    def bcast_scalars(col0, n, waits):
        mm(PS[:, 3584:3584 + n], ONES[:], SCL1[0:1, col0:col0 + n],
           banks=[7], start=True, stop=True,
           waits=list(waits) + [("ve", sch.val("ve")), ("si", t_eye)])
        return ps_read("scalar", lambda e: e.activation(
            out=SCL[:, col0:col0 + n], in_=PS[:, 3584:3584 + n],
            func=AF.Copy), banks=[7], small=True)

    def p1_src_hbm(t):
        buf = t % 2
        dst = P1X[:, buf * 1024:(buf + 1) * 1024]
        ws = [("se", sch.val("se")), ("ve", sch.val("ve")), ("po", sch.val("po"))]
        sem = "sx%d" % buf
        tload = sch.dma("sync", dst, x_in[t * 128:(t + 1) * 128, :], sem,
                        waits=ws)
        return dst, [(sem, tload)]

    ln_stats(p1_src_hbm, with_max=True)
    tred = ln_finalize(with_max=True)
    tgm = partition_max_to_scalar(RMAX[:], tred)
    tcc1 = global_max(0, tgm)

    def quantize_rows(n, fold_scale, rs_dst=None, rs_col=None, par=0):
        pw = PREPW[:, par * 2816:par * 2816 + n]
        pb = PREPB[:, par * 2816:par * 2816 + n]
        sch.emit("vector", lambda e, pw=pw: e.tensor_reduce(
            out=PREPR[:], in_=pw, axis=AX, op=ALU.max,
            apply_absolute_value=True), inc="ve", small=True)
        if rs_dst is not None:
            sch.emit("vector", lambda e: e.tensor_scalar(
                out=rs_dst[:, rs_col:rs_col + 1], in0=PREPR[:],
                scalar1=1.0 / QMAX, scalar2=None, op0=ALU.mult),
                inc="ve", small=True)
        sch.emit("vector", lambda e: e.reciprocal(out=HMC[:], in_=PREPR[:]),
                 inc="ve", small=True)
        sch.emit("vector", lambda e: e.tensor_scalar(
            out=HMC[:], in0=HMC[:], scalar1=QMAX, scalar2=None, op0=ALU.mult),
            inc="ve", small=True)
        sch.emit("vector", lambda e, pw=pw: e.tensor_scalar(
            out=PREPS[:, 0:n], in0=pw, scalar1=HMC[:], scalar2=MAGIC,
            op0=ALU.mult, op1=ALU.add), inc="ve")
        sch.emit("vector", lambda e: e.tensor_scalar(
            out=PREPS[:, 0:n], in0=PREPS[:, 0:n], scalar1=MAGIC, scalar2=QMAX,
            op0=ALU.subtract, op1=ALU.min), inc="ve")
        if fold_scale:
            sch.emit("vector", lambda e: e.tensor_scalar(
                out=PREPS[:, 0:n], in0=PREPS[:, 0:n], scalar1=-QMAX - 1.0,
                scalar2=None, op0=ALU.max), inc="ve")
            sch.emit("vector", lambda e: e.tensor_scalar(
                out=PREPR[:], in0=PREPR[:], scalar1=1.0 / QMAX, scalar2=None,
                op0=ALU.mult), inc="ve", small=True)
            return sch.emit("vector", lambda e, pb=pb: e.tensor_scalar(
                out=pb, in0=PREPS[:, 0:n], scalar1=PREPR[:],
                scalar2=None, op0=ALU.mult),
                waits=[("pe", prepb_war.get(par, 0))], inc="ve")
        return sch.emit("vector", lambda e, pb=pb: e.tensor_scalar(
            out=pb, in0=PREPS[:, 0:n], scalar1=-QMAX - 1.0,
            scalar2=None, op0=ALU.max),
            waits=[("pe", prepb_war.get(par, 0))], inc="ve")

    prepb_war = {}
    prepo_war = {}

    def prep_transpose_store(nblocks, dst_fn, tq, par=0, extra_src_waits=()):
        pbank = par * 2
        for g0 in range(0, nblocks, 8):
            nb = min(8, nblocks - g0)
            tpe = None
            for j in range(nb):
                b = g0 + j
                tpe = transpose(
                    PSB[:, pbank * 1024 + j * 128:pbank * 1024 + (j + 1) * 128],
                    PREPB[:, par * 2816 + b * 128:par * 2816 + (b + 1) * 128],
                    banks=[pbank, pbank + 1],
                    waits=[("ve", tq), ("si", t_eye), ("sgi", t_eye_g)]
                    + list(extra_src_waits))
            prepb_war[par] = tpe
            tev = ps_read("scalar", lambda e, nb=nb, pk=pbank: e.activation(
                out=PREPO[:, par * 2816:par * 2816 + nb * 128],
                in_=PSB[:, pk * 1024:pk * 1024 + nb * 128], func=AF.Copy),
                banks=[pbank, pbank + 1],
                waits=[("sq_st%d" % par, prepo_war.get(par, 0))])
            dst, want_3d = dst_fn(g0, nb)
            po_slice = PREPO[:, par * 2816:par * 2816 + nb * 128]
            src = (po_slice.rearrange("p (i k) -> p i k", i=nb)
                   if want_3d else po_slice)
            prepo_war[par] = sch.dma("scalar", dst, src, "sq_st%d" % par,
                                     waits=[("se", tev)])

    def prep_all():
        kq = 0
        prepw_war = {}
        for which, wsrc, rs in ((0, w_gate, RSG), (1, w_up, RSU)):
            for ic in range(N_IC):
                par = kq % 2
                kq += 1
                tw = sch.dma("sync", PREPW[:, par * 2816:par * 2816 + 1024],
                             wsrc[ic * 128:(ic + 1) * 128, :],
                             "swq%d" % par,
                             waits=[("ve", prepw_war.get(par, 0))])
                sch.emit("vector", lambda e: e.engine_nop(),
                         waits=[("swq%d" % par, tw)], inc="ve")
                tq = quantize_rows(1024, fold_scale=False, rs_dst=rs,
                                   rs_col=ic, par=par)
                prepw_war[par] = tq

                def dstf(g0, nb, ic=ic, which=which):
                    return (wgu_s[:, ic * 2048 + which * 1024:
                                  ic * 2048 + which * 1024 + 1024], False)
                prep_transpose_store(N_CC, dstf, tq, par=par)
        for ot in range(8):
            par = kq % 2
            kq += 1
            tw = sch.dma("sync", PREPW[:, par * 2816:par * 2816 + 2816],
                         w_down[ot * 128:(ot + 1) * 128, :], "swq%d" % par,
                         waits=[("ve", prepw_war.get(par, 0))])
            sch.emit("vector", lambda e: e.engine_nop(),
                     waits=[("swq%d" % par, tw)], inc="ve")
            tq = quantize_rows(2816, fold_scale=True, par=par)
            prepw_war[par] = tq

            def dstf(g0, nb, ot=ot):
                return (wd_s.rearrange("p (i o) -> p i o", i=N_IC)[
                    :, g0:g0 + nb, ot * 128:(ot + 1) * 128], True)
            prep_transpose_store(N_IC, dstf, tq, par=par)
        for et in range(24 + 8):
            par = kq % 2
            kq += 1
            if et < 24:
                src = w_attn[et * 128:(et + 1) * 128, :]
            else:
                src = w_proj[(et - 24) * 128:(et - 24 + 1) * 128, :]
            tw = sch.dma("gpsimd", PREPB[:, par * 2816:par * 2816 + 1024],
                         src, "sg_ld%d" % par,
                         waits=[("pe", prepb_war.get(par, 0))])

            def dstf(g0, nb, et=et):
                if et < 16:
                    return (wqk_s[:, et * 1024:(et + 1) * 1024], False)
                if et < 24:
                    vt = et - 16
                    return (wv_s.rearrange("p (c v k) -> p c v k",
                                           c=N_CC, v=8)[:, :, vt], True)
                ot = et - 24
                return (wp_s.rearrange("p (c o k) -> p c o k",
                                       c=N_CC, o=8)[:, :, ot], True)
            prep_transpose_store(N_CC, dstf, tq=sch.val("ve"), par=par,
                                 extra_src_waits=[("sg_ld%d" % par, tw)])

    prep_all()
    dump("wgu", wgu_s, [128, N_IC * 16 * 128], BF16)
    dump("wd", wd_s, [128, N_IC * D], BF16)
    dump("wqk", wqk_s, [128, 16 * D], BF16)
    dump("wv", wv_s, [128, N_CC * D], BF16)
    dump("wp", wp_s, [128, N_CC * D], BF16)
    dump("rsg", RSG[:], [128, N_IC], F32)
    dump("rsu", RSU[:], [128, N_IC], F32)

    if stop_after == "prep":
        return _finish(nc, sch, dbg_outs)

    def ffn(pos):
        if pos == 1:
            phase_barrier()
            ln_stats(lambda t: (XRES[:, t * 1024:(t + 1) * 1024], []),
                     with_max=True)
            tred = ln_finalize(with_max=True)
            tgm = partition_max_to_scalar(RMAX[:], tred)
            tb = global_max(0, tgm)
        else:
            tb = tcc1
        sch.emit("vector", lambda e: e.tensor_tensor(
            out=SCL1[0:1, 1:2], in0=SIGX[0:1, 0:1], in1=GMAXS[0:1, 0:1],
            op=ALU.mult), waits=[("sgc", tb), ("ve", t_sig)],
            inc="ve", small=True)
        sch.emit("vector", lambda e: e.tensor_scalar(
            out=SCL1[0:1, 1:2], in0=SCL1[0:1, 1:2], scalar1=1.0 / QMAX,
            scalar2=None, op0=ALU.mult), inc="ve", small=True)
        t3 = sch.emit("vector", lambda e: e.reciprocal(
            out=SCL1[0:1, 0:1], in_=SCL1[0:1, 1:2]), inc="ve", small=True)
        tbc = bcast_scalars(0, 2, [("ve", t3)])
        sch.emit("vector", lambda e: e.tensor_scalar(
            out=QA[:], in0=RSTD[:], scalar1=SCL[:, 0:1], scalar2=None,
            op0=ALU.mult), waits=[("se", tbc)], inc="ve", small=True)
        sch.emit("vector", lambda e: e.tensor_tensor(
            out=QB[:], in0=MEAN[:], in1=QA[:], op=ALU.mult),
            inc="ve", small=True)
        tqb = sch.emit("vector", lambda e: e.tensor_scalar(
            out=QB[:], in0=QB[:], scalar1=-1.0, scalar2=None, op0=ALU.mult),
            inc="ve", small=True)
        sch.emit("vector", lambda e: e.tensor_scalar(
            out=SGATE[:], in0=RSG[:], scalar1=SCL[:, 1:2], scalar2=None,
            op0=ALU.mult), waits=[("se", tbc)], inc="ve", small=True)
        tsg = sch.emit("vector", lambda e: e.tensor_scalar(
            out=SUP[:], in0=RSU[:], scalar1=SCL[:, 1:2], scalar2=None,
            op0=ALU.mult), inc="ve", small=True)
        tmx = sch.emit("vector", lambda e: e.memset(RMAX[:], 0.0),
                       inc="ve", small=True)

        phase_barrier()
        gev_war = {}
        hev_war = {}
        hsp_war = {}
        wgu_tok = {}
        xint_pe = {}
        xqtm_pe = 0
        n_st = p3_st if p3_st is not None else N_FFN_ST
        n_ic = p3_ic if p3_ic is not None else N_IC
        for st in range(n_st):
            spar = st % 2
            xoff = spar * N_CC * FFN_ST
            t_xint_se = None
            for tt in range(FFN_ST // 128):
                t = st * (FFN_ST // 128) + tt
                if pos == 0:
                    buf = tt % 2
                    xr = XTILE[:, buf * 1024:(buf + 1) * 1024]
                    sem = "sx%d" % buf
                    tload = sch.dma("sync", xr, x_in[t * 128:(t + 1) * 128, :],
                                    sem, waits=[("ve", sch.val("ve"))])
                    wsrc = [(sem, tload)]
                else:
                    xr = XRES[:, t * 1024:(t + 1) * 1024]
                    wsrc = []
                sch.emit("vector", lambda e, xr=xr, t=t: e.tensor_scalar(
                    out=QTMP[:], in0=xr, scalar1=QA[:, t:t + 1],
                    scalar2=QB[:, t:t + 1], op0=ALU.mult, op1=ALU.add),
                    waits=wsrc + [("ve", tqb), ("se", sch.val("se"))], inc="ve")
                sch.emit("vector", lambda e: e.tensor_scalar(
                    out=QTMP[:], in0=QTMP[:], scalar1=MAGIC, scalar2=MAGIC,
                    op0=ALU.add, op1=ALU.subtract), inc="ve")
                tq = sch.emit("vector", lambda e: e.tensor_scalar(
                    out=XQTM[:], in0=QTMP[:], scalar1=QMAX, scalar2=-QMAX - 1.0,
                    op0=ALU.min, op1=ALU.max),
                    waits=[("pe", xqtm_pe)], inc="ve")
                bank = 6 + (tt % 2)
                for b in range(N_CC):
                    xqtm_pe = transpose(
                        PSB[:, bank * 1024 + b * 128:bank * 1024 + (b + 1) * 128],
                        XQTM[:, b * 128:(b + 1) * 128], banks=[bank],
                        waits=[("ve", tq), ("sgi", t_eye_g)])
                dst = XINT[:, xoff:xoff + N_CC * FFN_ST].rearrange(
                    "p (c k) -> p c k", c=N_CC)[:, :, tt * 128:(tt + 1) * 128]
                srcp = PSB[:, bank * 1024:(bank + 1) * 1024].rearrange(
                    "p (c k) -> p c k", c=N_CC)
                t_xint_se = ps_read("scalar",
                                    lambda e, d=dst, s=srcp: e.activation(
                                        out=d, in_=s, func=AF.Copy),
                                    banks=[bank],
                                    waits=[("pe", xint_pe.get(spar, 0))])
            for ic in range(n_ic):
                wbuf = ic % 2
                wsem = "sw%d" % wbuf
                tw = sch.dma("sync", WGUB[:, wbuf * 2048:(wbuf + 1) * 2048],
                             wgu_s[:, ic * 2048:(ic + 1) * 2048], wsem,
                             waits=[("pe", wgu_tok.get(wbuf, 0))])
                par = ic % 2
                gbanks = [0, 1] if par == 0 else [2, 3]
                ubanks = [4, 5] if par == 0 else [6, 7]
                mm_w = [(wsem, tw), ("se", t_xint_se)]
                tok = None
                for base, banks in ((0, gbanks), (1024, ubanks)):
                    for c in range(N_CC):
                        lhs = WGUB[:, wbuf * 2048 + base + c * 128:
                                   wbuf * 2048 + base + (c + 1) * 128]
                        for half in range(2):
                            rhs = XINT[:, xoff + c * FFN_ST + half * 512:
                                       xoff + c * FFN_ST + half * 512 + 512]
                            tok = mm(PS[:, banks[half] * 512:
                                        banks[half] * 512 + 512],
                                     lhs, rhs, banks=[banks[half]],
                                     start=(c == 0), stop=(c == N_CC - 1),
                                     waits=mm_w)
                wgu_tok[wbuf] = tok
                xint_pe[spar] = tok
                gbuf = ic % 2
                hbuf = ic % 2
                gslc = GEV[:, gbuf * 1024:(gbuf + 1) * 1024]
                hslc = HEV[:, hbuf * 1024:(hbuf + 1) * 1024]
                tsilu = ps_read("scalar",
                                lambda e, g=gslc, ic=ic, b=gbanks: e.activation(
                                    out=g,
                                    in_=PS[:, b[0] * 512:b[0] * 512 + 1024],
                                    func=AF.Silu, scale=SGATE[:, ic:ic + 1]),
                                banks=gbanks,
                                waits=[("ve", tsg),
                                       ("ve", gev_war.get(gbuf, 0))])
                tstt = ps_read("vector",
                               lambda e, g=gslc, h=hslc, ic=ic, b=ubanks:
                                   e.scalar_tensor_tensor(
                                       out=h,
                                       in0=PS[:, b[0] * 512:b[0] * 512 + 1024],
                                       scalar=SUP[:, ic:ic + 1], in1=g,
                                       op0=ALU.mult, op1=ALU.mult),
                               banks=ubanks,
                               waits=[("se", tsilu), ("ve", tmx),
                                      ("po", hev_war.get(hbuf, 0))])
                gev_war[gbuf] = tstt
                sch.emit("vector", lambda e, h=hslc: e.tensor_reduce(
                    out=HMC[:], in_=h, axis=AX, op=ALU.max,
                    apply_absolute_value=True), inc="ve", small=True)
                sch.emit("vector", lambda e: e.tensor_tensor(
                    out=RMAX[:], in0=RMAX[:], in1=HMC[:], op=ALU.max),
                    inc="ve", small=True)
                tcast = sch.emit("gpsimd",
                                 lambda e, h=hslc, b=hbuf: e.tensor_copy(
                                     HSP[:, b * 1024:(b + 1) * 1024], h),
                                 waits=[("ve", tstt),
                                        ("sq%d" % hbuf, hsp_war.get(hbuf, 0))],
                                 inc="po")
                hev_war[hbuf] = tcast
                tspill = sch.dma("scalar",
                                 h_s[:, ic * T_CORE + st * FFN_ST:
                                     ic * T_CORE + (st + 1) * FFN_ST],
                                 HSP[:, hbuf * 1024:(hbuf + 1) * 1024],
                                 "sq%d" % hbuf, waits=[("po", tcast)])
                hsp_war[hbuf] = tspill

        if pos == 0:
            dump("h1", h_s, [128, N_IC * T_CORE], BF16)
            dump("s1a", SCL[:, 0:4], [128, 4], F32)
            dump("qa", QA[:], [128, N_TILES], F32)
            dump("qb", QB[:], [128, N_TILES], F32)
        if stop_after == ("ffn%d_p3" % (pos + 1)):
            return None

        tgm2 = partition_max_to_scalar(RMAX[:], sch.val("ve"))
        tb2 = global_max(1, tgm2)
        sch.emit("vector", lambda e: e.tensor_tensor(
            out=SCL1[0:1, 3:4], in0=SIGX[0:1, 1:2], in1=GMAXS[0:1, 1:2],
            op=ALU.mult), waits=[("sgc", tb2)], inc="ve", small=True)
        sch.emit("vector", lambda e: e.tensor_scalar(
            out=SCL1[0:1, 3:4], in0=SCL1[0:1, 3:4], scalar1=1.0 / QMAX,
            scalar2=None, op0=ALU.mult), inc="ve", small=True)
        sch.emit("vector", lambda e: e.reciprocal(
            out=SCL1[0:1, 2:3], in_=SCL1[0:1, 3:4]), inc="ve", small=True)
        t5 = sch.emit("vector", lambda e: e.tensor_scalar(
            out=SCL1[0:1, 3:4], in0=SCL1[0:1, 3:4], scalar1=0.5, scalar2=None,
            op0=ALU.mult), inc="ve", small=True)
        tbc2 = bcast_scalars(2, 2, [("ve", t5)])
        if pos == 0:
            dump("s2a", SCL[:, 0:4], [128, 4], F32)

        phase_barrier()
        twd = sch.dma("sync", WDT[:], wd_s, "swd")
        hq_pe = {}
        hraw_war = {}
        res_tok = {}
        out_toks = []
        h_s3 = h_s.rearrange("p (i t) -> p i t", i=N_IC)
        for st in range(N_P5_ST):
            spar = st % 2
            hoff = spar * N_IC * P5_ST
            tq_ic = {}
            for ic0 in range(0, N_IC, 2):
                nic = min(2, N_IC - ic0)
                w = nic * P5_ST
                buf = (ic0 // 2) % 2
                tl = sch.dma("gpsimd",
                             HRAW[:, buf * 512:buf * 512 + w],
                             h_s3[:, ic0:ic0 + nic,
                                  st * P5_ST:(st + 1) * P5_ST],
                             "sg%d" % buf,
                             waits=[("ve", hraw_war.get(buf, 0))])
                sch.emit("vector", lambda e, b=buf, w=w: e.tensor_scalar(
                    out=P5S[:, 0:w], in0=HRAW[:, b * 512:b * 512 + w],
                    scalar1=SCL[:, 2:3], scalar2=MAGIC, op0=ALU.mult,
                    op1=ALU.add),
                    waits=[("sg%d" % buf, tl), ("se", tbc2),
                           ("pe", hq_pe.get(spar, 0))],
                    inc="ve")
                sch.emit("vector", lambda e, w=w: e.tensor_scalar(
                    out=P5S[:, 0:w], in0=P5S[:, 0:w], scalar1=MAGIC,
                    scalar2=QMAX, op0=ALU.subtract, op1=ALU.min), inc="ve")
                tq = sch.emit("vector", lambda e, ic0=ic0, ho=hoff, w=w:
                              e.tensor_scalar(
                    out=HQ[:, ho + ic0 * P5_ST:ho + ic0 * P5_ST + w],
                    in0=P5S[:, 0:w],
                    scalar1=-QMAX - 1.0, scalar2=None, op0=ALU.max), inc="ve")
                for k in range(nic):
                    tq_ic[ic0 + k] = tq
                hraw_war[buf] = tq
            for tb_i in range(P5_ST // 128):
                t = st * (P5_ST // 128) + tb_i
                pair = t % 2
                banks = [0, 1] if pair == 0 else [2, 3]
                if pos == 0:
                    rsem = "sr%d" % (t % 2)
                    tload = sch.dma("sync", XRES[:, t * 1024:(t + 1) * 1024],
                                    x_in[t * 128:(t + 1) * 128, :], rsem,
                                    waits=[("ve", res_tok.get(t % 2, 0))])
                    res_w = [(rsem, tload)]
                else:
                    res_w = []
                tok = None
                for ic in range(N_IC):
                    lhs = HQ[:, hoff + ic * P5_ST + tb_i * 128:
                             hoff + ic * P5_ST + (tb_i + 1) * 128]
                    for half in range(2):
                        rhs = WDT[:, ic * D + half * 512:
                                  ic * D + half * 512 + 512]
                        tok = mm(PS[:, banks[half] * 512:banks[half] * 512 + 512],
                                 lhs, rhs, banks=[banks[half]],
                                 start=(ic == 0), stop=(ic == N_IC - 1),
                                 waits=[("ve", tq_ic[ic]), ("swd", twd)])
                hq_pe[spar] = tok
                xslc = XRES[:, t * 1024:(t + 1) * 1024]
                tev = ps_read("vector",
                              lambda e, xs=xslc, b=banks:
                                  e.scalar_tensor_tensor(
                                      out=xs,
                                      in0=PS[:, b[0] * 512:b[0] * 512 + 1024],
                                      scalar=SCL[:, 3:4], in1=xs,
                                      op0=ALU.mult, op1=ALU.add),
                              banks=banks, waits=res_w + [("se", tbc2)])
                if pos == 0:
                    res_tok[t % 2] = tev
                if pos == 1:
                    tst = sch.dma("scalar", out[t * 128:(t + 1) * 128, :],
                                  xslc, "so", waits=[("ve", tev)])
                    out_toks.append(tst)
        return out_toks

    ffn(0)
    dump("x1", XRES[:], [128, N_TILES * 1024], F32)
    if stop_after in ("ffn1_p3", "ffn1"):
        return _finish(nc, sch, dbg_outs)

    def attention():
        phase_barrier()
        ln_stats(lambda t: (XRES[:, t * 1024:(t + 1) * 1024], []),
                 with_max=False)
        ln_finalize(with_max=False)
        sch.emit("vector", lambda e: e.tensor_tensor(
            out=QB[:], in0=MEAN[:], in1=RSTD[:], op=ALU.mult),
            waits=[("se", sch.val("se"))], inc="ve", small=True)
        tnm = sch.emit("vector", lambda e: e.tensor_scalar(
            out=QB[:], in0=QB[:], scalar1=-1.0, scalar2=None, op0=ALU.mult),
            inc="ve", small=True)

        ysb_pe = 0
        xlnf_pe = 0
        xln_pe = 0
        wqk_tok = {}
        wv_tok = {}
        wp_tok = {}
        norm_toks = {}
        n_ast = att_st if att_st is not None else N_ATT_ST

        def emit_ty(r, qt, g, tnrm, ebuf):
            nonlocal ysb_pe
            wloc = qt // 2
            for h8 in range(8):
                for jc in range(2):
                    off = h8 * 256 + jc * 128
                    transpose(
                        PSB[:, 4 * 1024 + off:4 * 1024 + off + 128],
                        EXPS[:, ebuf + off:ebuf + off + 128],
                        banks=[4 + off // 1024],
                        waits=[("ve", tnrm), ("sgi", t_eye_g)])
            tevt = ps_read("vector", lambda e: e.tensor_copy(
                ATTT_I[:], PSI[:, 2048:3072]), banks=[4, 5])
            tp = (0, 64) if g else None
            for h8 in range(8):
                h = 2 * h8 + g
                for jc in range(2):
                    jb = wloc * 2 + jc
                    lhs = VS[:, jb * D + h * 64:jb * D + (h + 1) * 64]
                    rhs = ATTT[:, h8 * 256 + jc * 128:
                               h8 * 256 + (jc + 1) * 128]
                    oslc = PS[:, 3072 + h8 * 128:3072 + (h8 + 1) * 128]
                    oslc = (oslc[64:128, :] if g else oslc[0:64, :])
                    mm(oslc, lhs, rhs, banks=[6 + h8 // 4],
                       start=(jc == 0), stop=(jc == 1),
                       waits=[("ve", tevt)], tile_position=tp)
            dsty = YSB.rearrange("p (pr k) -> p pr k", pr=NH // 2)[
                g * 64:g * 64 + 64, :, qt * 128:(qt + 1) * 128]
            srcy = PS[g * 64:g * 64 + 64, 3072:4096].rearrange(
                "p (pr k) -> p pr k", pr=8)
            ps_read("scalar", lambda e, d=dsty, s=srcy: e.activation(
                out=d, in_=s, func=AF.Copy),
                banks=[6, 7], waits=[("pe", ysb_pe)])

        t_x_se_par = {}

        def emit_ln(st2):
            nonlocal xln_pe, xlnf_pe
            lpar = st2 % 2
            loff = lpar * N_CC * ATT_ST
            t_x_se = None
            for tt in range(ATT_ST // 128):
                t = st2 * (ATT_ST // 128) + tt
                xs = XRES[:, t * 1024:(t + 1) * 1024]
                tln = sch.emit("vector", lambda e, xs=xs, t=t: e.tensor_scalar(
                    out=XLN[:], in0=xs, scalar1=RSTD[:, t:t + 1],
                    scalar2=QB[:, t:t + 1], op0=ALU.mult, op1=ALU.add),
                    waits=[("ve", tnm), ("se", sch.val("se")),
                           ("pe", xln_pe)], inc="ve")
                bank = 6 + (tt % 2)
                for b in range(N_CC):
                    xln_pe = transpose(
                        PSB[:, bank * 1024 + b * 128:bank * 1024 + (b + 1) * 128],
                        XLN[:, b * 128:(b + 1) * 128], banks=[bank],
                        waits=[("ve", tln), ("sgi", t_eye_g)])
                dst = XLNF[:, loff:loff + N_CC * ATT_ST].rearrange(
                    "p (c k) -> p c k", c=N_CC)[:, :, tt * 128:(tt + 1) * 128]
                srcp = PSB[:, bank * 1024:(bank + 1) * 1024].rearrange(
                    "p (c k) -> p c k", c=N_CC)
                t_x_se = ps_read("scalar",
                                 lambda e, d=dst, s=srcp: e.activation(
                                     out=d, in_=s, func=AF.Copy),
                                 banks=[bank], waits=[("pe", xlnf_pe)])
            t_x_se_par[lpar] = t_x_se

        emit_ln(0)
        for st in range(n_ast):
            lpar = st % 2
            loff = lpar * N_CC * ATT_ST
            XLNF_ST = XLNF[:, loff:loff + N_CC * ATT_ST]
            t_x_se = t_x_se_par[lpar]
            for grp4 in range(4):
                for j in range(4):
                    qkc = grp4 * 4 + j
                    wbuf = qkc % 2
                    wsem = "sw%d" % wbuf
                    tw = sch.dma("sync", WQKB[:, wbuf * 1024:(wbuf + 1) * 1024],
                                 wqk_s[:, qkc * 1024:(qkc + 1) * 1024], wsem,
                                 waits=[("pe", wqk_tok.get(wbuf, 0))])
                    tok = None
                    for c in range(N_CC):
                        lhs = WQKB[:, wbuf * 1024 + c * 128:
                                   wbuf * 1024 + (c + 1) * 128]
                        rhs = XLNF_ST[:, c * ATT_ST:(c + 1) * ATT_ST]
                        tok = mm(PS[:, j * 512:j * 512 + 512], lhs, rhs,
                                 banks=[j], start=(c == 0),
                                 stop=(c == N_CC - 1),
                                 waits=[(wsem, tw), ("se", t_x_se)])
                    wqk_tok[wbuf] = tok
                    xlnf_pe = max(xlnf_pe, tok)
                scale = 0.125 if grp4 < 2 else 1.0
                ps_read("scalar", lambda e, g4=grp4, s=scale: e.activation(
                    out=QKS[:, g4 * 2048:(g4 + 1) * 2048],
                    in_=PS[:, 0:2048], func=AF.Copy, scale=np.float32(s)),
                    banks=[0, 1, 2, 3])
            for c in range(N_CC):
                wbuf = c % 2
                vsem = "sv%d" % wbuf
                tw = sch.dma("sync", WVB[:, wbuf * 1024:(wbuf + 1) * 1024],
                             wv_s[:, c * D:(c + 1) * D], vsem,
                             waits=[("pe", wv_tok.get(wbuf, 0))])
                tok = None
                for tb_i in range(4):
                    lhs = XLNF_ST[:, c * ATT_ST + tb_i * 128:
                                  c * ATT_ST + (tb_i + 1) * 128]
                    for half in range(2):
                        bank = tb_i * 2 + half
                        rhs = WVB[:, wbuf * 1024 + half * 512:
                                  wbuf * 1024 + (half + 1) * 512]
                        tok = mm(PS[:, bank * 512:bank * 512 + 512], lhs, rhs,
                                 banks=[bank], start=(c == 0),
                                 stop=(c == N_CC - 1),
                                 waits=[(vsem, tw), ("se", t_x_se)])
                wv_tok[wbuf] = tok
            for tb_i in range(4):
                ps_read("scalar", lambda e, tb_i=tb_i: e.activation(
                    out=VS[:, tb_i * D:(tb_i + 1) * D],
                    in_=PS[:, tb_i * 1024:(tb_i + 1) * 1024], func=AF.Copy),
                    banks=[tb_i * 2, tb_i * 2 + 1])
            if st + 1 < n_ast:
                emit_ln(st + 1)
            rounds = [(qt, g) for qt in range(4) for g in range(2)]
            prev = None
            for r0, (qt, g) in enumerate(rounds):
                r = st * 8 + r0
                wloc = qt // 2
                ebuf = (r % 2) * 2048
                tnrm = None
                for hf in range(2):
                    for h8 in range(hf * 4, hf * 4 + 4):
                        qc = h8
                        lhs = QKS[:, qc * ATT_ST + qt * 128:
                                  qc * ATT_ST + (qt + 1) * 128][
                            g * 64:g * 64 + 64, :]
                        rhs = QKS[:, (8 + qc) * ATT_ST + wloc * WIN:
                                  (8 + qc) * ATT_ST + (wloc + 1) * WIN][
                            g * 64:g * 64 + 64, :]
                        mm(PS[:, h8 * 256:h8 * 256 + 256], lhs, rhs,
                           banks=[h8 // 2], start=True, stop=True)
                    hb = [2 * hf, 2 * hf + 1]
                    eoff = ebuf + hf * 1024
                    texp = ps_read("scalar", lambda e, eo=eoff, hf=hf:
                                   e.activation(
                        out=EXPS[:, eo:eo + 1024],
                        in_=PS[:, hf * 1024:hf * 1024 + 1024], func=AF.Exp,
                        bias=np.float32(0.0), scale=np.float32(1.0)),
                        banks=hb,
                        waits=[("ve", norm_toks.get((r - 2, hf), 0))])
                    sch.emit("vector", lambda e, eo=eoff, hf=hf: e.tensor_reduce(
                        out=SUMS[:, hf * 4:hf * 4 + 4],
                        in_=EXPS[:, eo:eo + 1024].rearrange(
                            "p (h j) -> p h j", h=4), axis=AX, op=ALU.add),
                        waits=[("se", texp)], inc="ve", small=True)
                    trec = sch.emit("vector", lambda e, hf=hf: e.reciprocal(
                        out=RECS[:, hf * 4:hf * 4 + 4],
                        in_=SUMS[:, hf * 4:hf * 4 + 4]), inc="ve", small=True)
                    tnrm = sch.emit("vector", lambda e, eo=eoff, hf=hf:
                                    e.tensor_tensor(
                        out=EXPS[:, eo:eo + 1024].rearrange(
                            "p (h j) -> p h j", h=4),
                        in0=EXPS[:, eo:eo + 1024].rearrange(
                            "p (h j) -> p h j", h=4),
                        in1=RECS[:, hf * 4:hf * 4 + 4].rearrange(
                            "p (h o) -> p h o", o=1)
                            .broadcast_to([128, 4, WIN]),
                        op=ALU.mult), waits=[("ve", trec)], inc="ve")
                    norm_toks[(r, hf)] = tnrm
                if prev is not None:
                    emit_ty(*prev)
                prev = (r, qt, g, tnrm, ebuf)
            emit_ty(*prev)
            t_y_se = sch.val("se")
            tokp = None
            for p in range(NH // 2):
                wbuf = p % 2
                psem = "sv%d" % wbuf
                tw = sch.dma("sync", WPB[:, wbuf * 1024:(wbuf + 1) * 1024],
                             wp_s[:, p * D:(p + 1) * D], psem,
                             waits=[("pe", wv_tok.get(wbuf, 0))])
                for tb_i in range(4):
                    lhs = YSB[:, p * ATT_ST + tb_i * 128:
                              p * ATT_ST + (tb_i + 1) * 128]
                    for half in range(2):
                        bank = tb_i * 2 + half
                        tokp = mm(PS[:, bank * 512:bank * 512 + 512], lhs,
                                  WPB[:, wbuf * 1024 + half * 512:
                                      wbuf * 1024 + (half + 1) * 512],
                                  banks=[bank], start=(p == 0),
                                  stop=(p == NH // 2 - 1),
                                  waits=[(psem, tw), ("se", t_y_se)])
                wv_tok[wbuf] = tokp
            ysb_pe = tokp
            for tb_i in range(4):
                t = st * 4 + tb_i
                banks = [tb_i * 2, tb_i * 2 + 1]
                xslc = XRES[:, t * 1024:(t + 1) * 1024]
                ps_read("vector", lambda e, xs=xslc, b=banks: e.tensor_tensor(
                    out=xs, in0=PS[:, b[0] * 512:b[0] * 512 + 1024], in1=xs,
                    op=ALU.add), banks=banks)

    attention()
    dump("x2", XRES[:], [128, N_TILES * 1024], F32)
    if stop_after == "attn":
        return _finish(nc, sch, dbg_outs)

    ffn(1)
    return _finish(nc, sch, dbg_outs)


def _finish(nc, sch, dbg_outs):
    final = sch.val("so")
    if final > 0:
        sch.emit("sync", lambda e: e.nop(), waits=[("so", final)])
    sch.build()
    return nc, dbg_outs


def _in_maps(inputs):
    x = np.ascontiguousarray(
        np.asarray(inputs["L0_attn_x"], dtype=np.float32).reshape(-1, D))
    shards = [x[r * T_CORE:(r + 1) * T_CORE] for r in range(N_CORES)]
    eye_f32 = np.eye(128, dtype=np.float32)
    base = dict(
        w_attn=np.ascontiguousarray(np.asarray(inputs["L0_attn_w_attn"], np.float32)),
        w_proj=np.ascontiguousarray(np.asarray(inputs["L0_attn_w_proj"], np.float32)),
        w_gate=np.ascontiguousarray(np.asarray(inputs["L0_ffn_w_gate"], np.float32)),
        w_up=np.ascontiguousarray(np.asarray(inputs["L0_ffn_w_up"], np.float32)),
        w_down=np.ascontiguousarray(np.asarray(inputs["L0_ffn_w_down"], np.float32)),
        clip_x=np.asarray(inputs["L0_ffn_clip_x"], np.float32).reshape(1, 1),
        clip_down=np.asarray(inputs["L0_ffn_clip_down"], np.float32).reshape(1, 1),
        eye_f32=eye_f32,
    )
    return [dict(base, x=shards[r]) for r in range(N_CORES)]


def kernel(**inputs):
    nc, _ = build_program()
    in_maps = _in_maps(inputs)
    res = run_bass_kernel_spmd(nc, in_maps, core_ids=list(range(N_CORES)))
    parts = [res.results[r]["out"] for r in range(N_CORES)]
    return np.concatenate(parts, axis=0).reshape(4, 8192, D).astype(np.float32)


# version 12
# speedup vs baseline: 1.0360x; 1.0360x over previous
from contextlib import ExitStack

import numpy as np
import ml_dtypes

import concourse.bass as bass
import concourse.mybir as mybir
from concourse.bass_utils import run_bass_kernel_spmd

F32 = mybir.dt.float32
BF16 = mybir.dt.bfloat16
I32 = mybir.dt.int32
AF = mybir.ActivationFunctionType
ALU = mybir.AluOpType
AX = mybir.AxisListType.X

MAGIC = float(np.float32(1.5 * 2 ** 23))
QMAX = 127.0

N_CORES = 8
T_CORE = 4096
D = 1024
I_FFN = 2816
NH = 16
HD = 64
WIN = 256
N_TILES = T_CORE // 128
N_IC = I_FFN // 128
N_CC = D // 128
FFN_ST = 1024
N_FFN_ST = T_CORE // FFN_ST
P5_ST = 256
N_P5_ST = T_CORE // P5_ST
ATT_ST = 512
N_ATT_ST = T_CORE // ATT_ST
EPS = 1e-5

ALL_SEMS = ("pe", "ve", "se", "po", "sx", "sw", "sg", "sq", "cc")


class Sched:

    ENGINES = ("tensor", "vector", "scalar", "gpsimd", "sync")

    def __init__(self, nc):
        self.nc = nc
        self.ops = {e: [] for e in self.ENGINES}
        self.es = ExitStack()
        self.sems = {}
        self.counts = {}
        self.waited = {}
        self.ps_w = [0] * 8
        self.ps_r = [dict() for _ in range(8)]
        self.pending_small = {}
        self.eng_sem = {"tensor": "pe", "vector": "ve", "scalar": "se",
                        "gpsimd": "po"}

    def sem(self, name):
        if name not in self.sems:
            self.sems[name] = self.es.enter_context(self.nc.semaphore(name))
            self.counts[name] = 0
        return self.sems[name]

    def emit(self, engine, fn, waits=(), inc=None, inc_amt=1, small=False):
        if inc:
            self.sem(inc)
        ws = []
        if engine in ("vector", "scalar", "gpsimd") and \
                inc == self.eng_sem.get(engine) and self.counts.get(inc, 0) > 0:
            ws.append((inc, self.counts[inc]))
        ps = self.pending_small.pop(engine, None)
        if ps is not None:
            ws.append(ps)
            self.sem(ps[0])
        for s, v in waits:
            self.sem(s)
            assert v <= self.counts[s], f"wait on future token {s}:{v}"
            if v > 0:
                ws.append((s, v))
        tok = None
        if inc is not None:
            self.counts[inc] += inc_amt
            tok = self.counts[inc]
        if small:
            assert inc == self.eng_sem.get(engine), \
                f"small op on {engine} must inc {self.eng_sem.get(engine)}"
            self.pending_small[engine] = (inc, tok)
        self.ops[engine].append((fn, tuple(ws), inc, inc_amt))
        return tok

    def val(self, semname):
        return self.counts.get(semname, 0)

    def dma(self, queue, out, in_, sem, waits=()):
        return self.emit(queue, lambda e, o=out, i=in_: e.dma_start(out=o, in_=i),
                         waits=waits, inc=sem, inc_amt=16)

    def ps_write_waits(self, banks):
        ws = []
        for b in banks:
            for s, v in self.ps_r[b].items():
                ws.append((s, v))
        return ws

    def ps_mark_write(self, banks, pe_tok):
        for b in banks:
            self.ps_w[b] = pe_tok
            self.ps_r[b] = {}

    def ps_read_waits(self, banks):
        return [("pe", max(self.ps_w[b] for b in banks))]

    def ps_mark_read(self, banks, sem, tok):
        for b in banks:
            self.ps_r[b][sem] = max(self.ps_r[b].get(sem, 0), tok)

    COMPUTE_SEMS = ("pe", "ve", "se", "po")

    def _coalesce_incs(self):
        waited_vals = {}
        for e in self.ENGINES:
            for _, waits, _, _ in self.ops[e]:
                for s, v in waits:
                    waited_vals.setdefault(s, set()).add(v)
        for e in self.ENGINES:
            newops = []
            running = {}
            pending = {}
            last_inc_idx = {}
            for fn, waits, inc, inc_amt in self.ops[e]:
                newops.append([fn, waits, None, 0])
                if inc is None or inc not in self.COMPUTE_SEMS:
                    if inc is not None:
                        newops[-1][2] = inc
                        newops[-1][3] = inc_amt
                    continue
                running[inc] = running.get(inc, 0) + inc_amt
                pending[inc] = pending.get(inc, 0) + inc_amt
                last_inc_idx[inc] = len(newops) - 1
                if running[inc] in waited_vals.get(inc, ()):
                    newops[-1][2] = inc
                    newops[-1][3] = pending[inc]
                    pending[inc] = 0
            for s, amt in pending.items():
                if amt:
                    idx = last_inc_idx[s]
                    if newops[idx][2] is None:
                        newops[idx][2] = s
                        newops[idx][3] = amt
                    else:
                        assert newops[idx][2] == s
                        newops[idx][3] += amt
            self.ops[e] = [tuple(t) for t in newops]

    def replay(self, engine, eng):
        for fn, waits, inc, inc_amt in self.ops[engine]:
            for s, v in waits:
                key = (engine, s)
                if v <= self.waited.get(key, 0):
                    continue
                eng.wait_ge(self.sems[s], v)
                self.waited[key] = v
            inst = fn(eng)
            if inc is not None and inc_amt > 0:
                assert inst is not None
                inst.then_inc(self.sems[inc], inc_amt)

    def build(self):
        self._coalesce_incs()
        with self.nc.Block() as blk:
            blk.tensor(lambda eng: self.replay("tensor", eng))
            blk.vector(lambda eng: self.replay("vector", eng))
            blk.scalar(lambda eng: self.replay("scalar", eng))
            blk.gpsimd(lambda eng: self.replay("gpsimd", eng))
            blk.sync(lambda eng: self.replay("sync", eng))
        self.es.close()


def build_program(dbg=(), stop_after=None, p3_st=None, p3_ic=None, att_st=None, att_rounds=True, att_lvl=3):
    dbg = set(dbg)
    nc = bass.Bass("TRN2", target_bir_lowering=False)

    x_in = nc.dram_tensor("x", [T_CORE, D], F32, kind="ExternalInput").ap()
    w_attn = nc.dram_tensor("w_attn", [3 * D, D], F32, kind="ExternalInput").ap()
    w_proj = nc.dram_tensor("w_proj", [D, D], F32, kind="ExternalInput").ap()
    w_gate = nc.dram_tensor("w_gate", [I_FFN, D], F32, kind="ExternalInput").ap()
    w_up = nc.dram_tensor("w_up", [I_FFN, D], F32, kind="ExternalInput").ap()
    w_down = nc.dram_tensor("w_down", [D, I_FFN], F32, kind="ExternalInput").ap()
    clip_x = nc.dram_tensor("clip_x", [1, 1], F32, kind="ExternalInput").ap()
    clip_down = nc.dram_tensor("clip_down", [1, 1], F32, kind="ExternalInput").ap()
    eyef_in = nc.dram_tensor("eye_f32", [128, 128], F32, kind="ExternalInput").ap()
    out = nc.dram_tensor("out", [T_CORE, D], F32, kind="ExternalOutput").ap()

    wgu_s = nc.dram_tensor("wgu_s", [128, N_IC * 16 * 128], BF16, kind="Internal").ap()
    wd_s = nc.dram_tensor("wd_s", [128, N_IC * D], BF16, kind="Internal").ap()
    wqk_s = nc.dram_tensor("wqk_s", [128, 16 * D], BF16, kind="Internal").ap()
    wv_s = nc.dram_tensor("wv_s", [128, N_CC * D], BF16, kind="Internal").ap()
    wp_s = nc.dram_tensor("wp_s", [128, N_CC * D], BF16, kind="Internal").ap()
    h_s = nc.dram_tensor("h_s", [128, N_IC * T_CORE], BF16, kind="Internal").ap()
    cc_in = nc.dram_tensor("cc_in", [1, 1], F32, kind="Internal").ap()
    cc_out = nc.dram_tensor("cc_out", [1, 1], F32, kind="Internal").ap()

    sch = Sched(nc)
    es = sch.es

    def sb(name, shape, dtype):
        return nc.alloc_sbuf_tensor(name, shape, dtype)

    XRES = sb("XRES", [128, N_TILES * 1024], F32)
    EYEB = sb("EYEB", [128, 128], BF16)
    EYEF = sb("EYEF", [128, 128], F32)
    ONES = sb("ONES", [1, 128], F32)
    SUM = sb("SUM", [128, N_TILES], F32)
    SSQ = sb("SSQ", [128, N_TILES], F32)
    MX = sb("MX", [128, N_TILES], F32)
    MN = sb("MN", [128, N_TILES], F32)
    MEAN = sb("MEAN", [128, N_TILES], F32)
    RSTD = sb("RSTD", [128, N_TILES], F32)
    QA = sb("QA", [128, N_TILES], F32)
    QB = sb("QB", [128, N_TILES], F32)
    T0 = sb("T0", [128, N_TILES], F32)
    T1 = sb("T1", [128, N_TILES], F32)
    RMAX = sb("RMAX", [128, 1], F32)
    HMC = sb("HMC", [128, 1], F32)
    SCL = sb("SCL", [128, 8], F32)
    SCL1 = sb("SCL1", [1, 8], F32)
    SGATE = sb("SGATE", [128, N_IC], F32)
    SUP = sb("SUP", [128, N_IC], F32)
    RSG = sb("RSG", [128, N_IC], F32)
    RSU = sb("RSU", [128, N_IC], F32)
    SIGX = sb("SIGX", [1, 2], F32)
    GMAXS = sb("GMAXS", [1, 2], F32)
    NEGM = sb("NEGM", [128, 1], F32)
    SUMS = sb("SUMS", [128, 8], F32)
    RECS = sb("RECS", [128, 8], F32)
    QTMP = sb("QTMP", [128, 1024], F32)

    ARENA_BYTES = 74240
    ARENA = nc.alloc_sbuf_tensor("ARENA", [128, ARENA_BYTES], mybir.dt.uint8)
    abase = nc.lookup_mloc(ARENA).addr

    def at(name, shape, dtype, off):
        nbytes = int(np.prod(shape[1:])) * mybir.dt.size(dtype)
        assert off + nbytes <= ARENA_BYTES, f"{name} overflows arena"
        return nc.alloc_sbuf_tensor_at(name, shape, dtype, offset=abase + off)

    PREPW = at("PREPW", [128, 2 * 2816], F32, 0)
    PREPS = at("PREPS", [128, 2816], F32, 22528)
    PREPB = at("PREPB", [128, 2 * 2816], BF16, 33792)
    PREPO = at("PREPO", [128, 2 * 2816], BF16, 45056)
    PREPR = at("PREPR", [128, 1], F32, 56320)
    P1X = at("P1X", [128, 2 * 1024], F32, 59392)
    XTILE = at("XTILE", [128, 2 * 1024], F32, 0)
    XQTM = at("XQTM", [128, 1024], BF16, 8192)
    XINT = at("XINT", [128, 2 * N_CC * FFN_ST], BF16, 10240)
    WGUB = at("WGUB", [128, 2 * 2048], BF16, 43008)
    GEV = at("GEV", [128, 2 * 1024], F32, 51200)
    HEV = at("HEV", [128, 2 * 1024], F32, 59392)
    HSP = at("HSP", [128, 2 * 1024], BF16, 67584)
    WDT = at("WDT", [128, N_IC * D], BF16, 0)
    HRAW = at("HRAW", [128, 2 * 2 * P5_ST], BF16, 45056)
    HQ = at("HQ", [128, 2 * N_IC * P5_ST], BF16, 47104)
    P5S = at("P5S", [128, 2 * P5_ST], F32, 69632)
    QKS = at("QKS", [128, 16 * ATT_ST], BF16, 0)
    VS = at("VS", [128, (ATT_ST // 128) * D], BF16, 16384)
    XLNF = at("XLNF", [128, 2 * N_CC * ATT_ST], BF16, 24576)
    XLN = at("XLN", [128, 1024], BF16, 40960)
    WQKB = at("WQKB", [128, 2 * 1024], BF16, 43008)
    WVB = at("WVB", [128, 2 * 1024], BF16, 47104)
    WPB = WVB
    EXPS = at("EXPS", [128, 2 * 8 * WIN], BF16, 51200)
    ATTT = at("ATTT", [128, 8 * WIN], BF16, 59392)
    YSB = at("YSB", [128, (NH // 2) * ATT_ST], BF16, 63488)

    PSH = nc.alloc_psum_tensor("PS", [128, 4096], F32)
    PS = PSH[:]
    PSB = PSH.bitcast(BF16)[:]
    PSI = PSH.bitcast(I32)[:]
    ATTT_I = ATTT.bitcast(I32)

    dbg_outs = {}

    def barrier_waits():
        return [(s, sch.val(s)) for s in list(sch.counts.keys())]

    def phase_barrier():
        w = barrier_waits()
        for eng in Sched.ENGINES:
            sch.emit(eng, lambda e: e.nop(), waits=w)

    def dump(name, src, shape, dtype=F32):
        if name not in dbg:
            return
        d = nc.dram_tensor("dbg_" + name, list(shape), dtype,
                           kind="ExternalOutput").ap()
        dbg_outs[name] = d
        sch.dma("sync", d, src, "so", waits=barrier_waits())

    def mm(out_ap, lhsT, rhs, banks, start, stop, waits=(), tile_position=None):
        ws = list(waits) + (sch.ps_write_waits(banks) if start else [])
        tok = sch.emit(
            "tensor",
            lambda e, o=out_ap, l=lhsT, r=rhs, s0=start, s1=stop, tp=tile_position:
                e.matmul(o, l, r, start=s0, stop=s1, tile_position=tp),
            waits=ws, inc="pe")
        sch.ps_mark_write(banks, tok)
        return tok

    def transpose(out_ap, in_ap, banks, waits=(), identity=None):
        idn = identity if identity is not None else EYEB[:]
        ws = list(waits) + sch.ps_write_waits(banks)
        tok = sch.emit(
            "tensor",
            lambda e, o=out_ap, i=in_ap, idn=idn: e.transpose(o, i, idn),
            waits=ws, inc="pe")
        sch.ps_mark_write(banks, tok)
        return tok

    def ps_read(engine, fn, banks, waits=(), small=False):
        sem = {"scalar": "se", "vector": "ve", "gpsimd": "po"}[engine]
        ws = list(waits) + sch.ps_read_waits(banks)
        tok = sch.emit(engine, fn, waits=ws, inc=sem, small=small)
        sch.ps_mark_read(banks, sem, tok)
        return tok

    sch.dma("gpsimd", EYEB[:], eyef_in, "sgi")
    sch.dma("sync", EYEF[:], eyef_in, "si")
    sch.dma("sync", SIGX[0:1, 0:1], clip_x, "si")
    sch.dma("sync", SIGX[0:1, 1:2], clip_down, "si")
    t_eye_g = sch.val("sgi")
    t_eye = sch.val("si")
    sch.emit("vector", lambda e: e.memset(ONES[:], 1.0), inc="ve")
    t = sch.val("si")
    sch.emit("scalar", lambda e: e.activation(out=SCL1[0:1, 0:2],
                                              in_=SIGX[0:1, 0:2],
                                              func=AF.Exp, scale=np.float32(-1.0)),
             waits=[("si", t)], inc="se", small=True)
    sch.emit("vector", lambda e: e.tensor_scalar(out=SCL1[0:1, 0:2],
                                                 in0=SCL1[0:1, 0:2], scalar1=1.0,
                                                 scalar2=None, op0=ALU.add),
             waits=[("se", sch.val("se"))], inc="ve", small=True)
    t_sig = sch.emit("vector", lambda e: e.reciprocal(out=SIGX[0:1, 0:2],
                                                      in_=SCL1[0:1, 0:2]),
                     inc="ve", small=True)

    def ln_stats(tile_src_fn, with_max):
        for t in range(N_TILES):
            src, waits = tile_src_fn(t)
            sch.emit("scalar",
                     lambda e, s=src, t=t: e.activation(
                         out=QTMP[:], in_=s, func=AF.Square,
                         accum_out=SSQ[:, t:t + 1]),
                     waits=waits, inc="se", small=True)
            sch.emit("vector",
                     lambda e, s=src, t=t: e.tensor_reduce(
                         out=SUM[:, t:t + 1], in_=s, axis=AX, op=ALU.add),
                     waits=waits, inc="ve", small=True)
            if with_max:
                sch.emit("vector",
                         lambda e, s=src, t=t: e.tensor_reduce(
                             out=MX[:, t:t + 1], in_=s, axis=AX, op=ALU.max),
                         waits=waits, inc="ve", small=True)
                sch.emit("vector",
                         lambda e, s=src, t=t: e.tensor_reduce(
                             out=MN[:, t:t + 1], in_=s, axis=AX, op=ALU.min),
                         waits=waits, inc="ve", small=True)

    def ln_finalize(with_max):
        sl = slice(0, N_TILES)
        w = [("se", sch.val("se")), ("ve", sch.val("ve")), ("po", sch.val("po"))]
        sch.emit("vector", lambda e: e.tensor_scalar(
            out=MEAN[:, sl], in0=SUM[:, sl], scalar1=1.0 / D, scalar2=None,
            op0=ALU.mult), waits=w, inc="ve", small=True)
        sch.emit("vector", lambda e: e.tensor_scalar(
            out=T0[:, sl], in0=SSQ[:, sl], scalar1=1.0 / D, scalar2=None,
            op0=ALU.mult), inc="ve", small=True)
        sch.emit("vector", lambda e: e.tensor_tensor(
            out=T1[:, sl], in0=MEAN[:, sl], in1=MEAN[:, sl], op=ALU.mult),
            inc="ve", small=True)
        tvar = sch.emit("vector", lambda e: e.tensor_tensor(
            out=T0[:, sl], in0=T0[:, sl], in1=T1[:, sl], op=ALU.subtract),
            inc="ve", small=True)
        sch.emit("scalar", lambda e: e.activation(
            out=T1[:, sl], in_=T0[:, sl], func=AF.Ln, bias=np.float32(EPS)),
            waits=[("ve", tvar)], inc="se", small=True)
        trstd = sch.emit("scalar", lambda e: e.activation(
            out=RSTD[:, sl], in_=T1[:, sl], func=AF.Exp, scale=np.float32(-0.5)),
            inc="se", small=True)
        if not with_max:
            return trstd
        sch.emit("vector", lambda e: e.tensor_tensor(
            out=T0[:, sl], in0=MX[:, sl], in1=MEAN[:, sl], op=ALU.subtract),
            waits=[("se", trstd)], inc="ve", small=True)
        sch.emit("vector", lambda e: e.tensor_tensor(
            out=T1[:, sl], in0=MEAN[:, sl], in1=MN[:, sl], op=ALU.subtract),
            inc="ve", small=True)
        sch.emit("vector", lambda e: e.tensor_tensor(
            out=T0[:, sl], in0=T0[:, sl], in1=T1[:, sl], op=ALU.max),
            inc="ve", small=True)
        sch.emit("vector", lambda e: e.tensor_tensor(
            out=T0[:, sl], in0=T0[:, sl], in1=RSTD[:, sl], op=ALU.mult),
            inc="ve", small=True)
        tred = sch.emit("vector", lambda e: e.tensor_reduce(
            out=RMAX[:], in_=T0[:, sl], axis=AX, op=ALU.max),
            inc="ve", small=True)
        return tred

    def partition_max_to_scalar(src_col, tred):
        transpose(PS[0:1, 3584:3712], src_col, banks=[7],
                  waits=[("ve", tred), ("si", t_eye)], identity=EYEF[:])
        tgm = ps_read("vector", lambda e: e.tensor_reduce(
            out=T0[0:1, 0:1], in_=PS[0:1, 3584:3712], axis=AX, op=ALU.max),
            banks=[7], small=True)
        return tgm

    def global_max(idx, tgm):
        ts = sch.dma("gpsimd", cc_in, T0[0:1, 0:1], "sgc",
                     waits=[("ve", tgm), ("cc", sch.val("cc"))])
        tcc = sch.emit("gpsimd",
                       lambda e: e.collective_compute(
                           "AllReduce", ALU.max,
                           replica_groups=[list(range(N_CORES))],
                           ins=[cc_in.opt()], outs=[cc_out.opt()]),
                       waits=[("sgc", ts)], inc="cc")
        tb = sch.dma("gpsimd", GMAXS[0:1, idx:idx + 1], cc_out, "sgc",
                     waits=[("cc", tcc)])
        return tb

    def bcast_scalars(col0, n, waits):
        mm(PS[:, 3584:3584 + n], ONES[:], SCL1[0:1, col0:col0 + n],
           banks=[7], start=True, stop=True,
           waits=list(waits) + [("ve", sch.val("ve")), ("si", t_eye)])
        return ps_read("scalar", lambda e: e.activation(
            out=SCL[:, col0:col0 + n], in_=PS[:, 3584:3584 + n],
            func=AF.Copy), banks=[7], small=True)

    def p1_src_hbm(t):
        buf = t % 2
        dst = P1X[:, buf * 1024:(buf + 1) * 1024]
        ws = [("se", sch.val("se")), ("ve", sch.val("ve")), ("po", sch.val("po"))]
        sem = "sx%d" % buf
        tload = sch.dma("sync", dst, x_in[t * 128:(t + 1) * 128, :], sem,
                        waits=ws)
        return dst, [(sem, tload)]

    ln_stats(p1_src_hbm, with_max=True)
    tred = ln_finalize(with_max=True)
    tgm = partition_max_to_scalar(RMAX[:], tred)
    tcc1 = global_max(0, tgm)

    def quantize_rows(n, fold_scale, rs_dst=None, rs_col=None, par=0):
        pw = PREPW[:, par * 2816:par * 2816 + n]
        pb = PREPB[:, par * 2816:par * 2816 + n]
        sch.emit("vector", lambda e, pw=pw: e.tensor_reduce(
            out=PREPR[:], in_=pw, axis=AX, op=ALU.max,
            apply_absolute_value=True), inc="ve", small=True)
        if rs_dst is not None:
            sch.emit("vector", lambda e: e.tensor_scalar(
                out=rs_dst[:, rs_col:rs_col + 1], in0=PREPR[:],
                scalar1=1.0 / QMAX, scalar2=None, op0=ALU.mult),
                inc="ve", small=True)
        sch.emit("vector", lambda e: e.reciprocal(out=HMC[:], in_=PREPR[:]),
                 inc="ve", small=True)
        sch.emit("vector", lambda e: e.tensor_scalar(
            out=HMC[:], in0=HMC[:], scalar1=QMAX, scalar2=None, op0=ALU.mult),
            inc="ve", small=True)
        sch.emit("vector", lambda e, pw=pw: e.tensor_scalar(
            out=PREPS[:, 0:n], in0=pw, scalar1=HMC[:], scalar2=MAGIC,
            op0=ALU.mult, op1=ALU.add), inc="ve")
        sch.emit("vector", lambda e: e.tensor_scalar(
            out=PREPS[:, 0:n], in0=PREPS[:, 0:n], scalar1=MAGIC, scalar2=QMAX,
            op0=ALU.subtract, op1=ALU.min), inc="ve")
        if fold_scale:
            sch.emit("vector", lambda e: e.tensor_scalar(
                out=PREPS[:, 0:n], in0=PREPS[:, 0:n], scalar1=-QMAX - 1.0,
                scalar2=None, op0=ALU.max), inc="ve")
            sch.emit("vector", lambda e: e.tensor_scalar(
                out=PREPR[:], in0=PREPR[:], scalar1=1.0 / QMAX, scalar2=None,
                op0=ALU.mult), inc="ve", small=True)
            return sch.emit("vector", lambda e, pb=pb: e.tensor_scalar(
                out=pb, in0=PREPS[:, 0:n], scalar1=PREPR[:],
                scalar2=None, op0=ALU.mult),
                waits=[("pe", prepb_war.get(par, 0))], inc="ve")
        return sch.emit("vector", lambda e, pb=pb: e.tensor_scalar(
            out=pb, in0=PREPS[:, 0:n], scalar1=-QMAX - 1.0,
            scalar2=None, op0=ALU.max),
            waits=[("pe", prepb_war.get(par, 0))], inc="ve")

    prepb_war = {}
    prepo_war = {}

    def prep_transpose_store(nblocks, dst_fn, tq, par=0, extra_src_waits=()):
        pbank = par * 2
        for g0 in range(0, nblocks, 8):
            nb = min(8, nblocks - g0)
            tpe = None
            for j in range(nb):
                b = g0 + j
                tpe = transpose(
                    PSB[:, pbank * 1024 + j * 128:pbank * 1024 + (j + 1) * 128],
                    PREPB[:, par * 2816 + b * 128:par * 2816 + (b + 1) * 128],
                    banks=[pbank, pbank + 1],
                    waits=[("ve", tq), ("si", t_eye), ("sgi", t_eye_g)]
                    + list(extra_src_waits))
            prepb_war[par] = tpe
            tev = ps_read("scalar", lambda e, nb=nb, pk=pbank: e.activation(
                out=PREPO[:, par * 2816:par * 2816 + nb * 128],
                in_=PSB[:, pk * 1024:pk * 1024 + nb * 128], func=AF.Copy),
                banks=[pbank, pbank + 1],
                waits=[("sq_st%d" % par, prepo_war.get(par, 0))])
            dst, want_3d = dst_fn(g0, nb)
            po_slice = PREPO[:, par * 2816:par * 2816 + nb * 128]
            src = (po_slice.rearrange("p (i k) -> p i k", i=nb)
                   if want_3d else po_slice)
            prepo_war[par] = sch.dma("scalar", dst, src, "sq_st%d" % par,
                                     waits=[("se", tev)])

    def prep_all():
        kq = 0
        prepw_war = {}
        for which, wsrc, rs in ((0, w_gate, RSG), (1, w_up, RSU)):
            for ic in range(N_IC):
                par = kq % 2
                kq += 1
                tw = sch.dma("sync", PREPW[:, par * 2816:par * 2816 + 1024],
                             wsrc[ic * 128:(ic + 1) * 128, :],
                             "swq%d" % par,
                             waits=[("ve", prepw_war.get(par, 0))])
                sch.emit("vector", lambda e: e.engine_nop(),
                         waits=[("swq%d" % par, tw)], inc="ve")
                tq = quantize_rows(1024, fold_scale=False, rs_dst=rs,
                                   rs_col=ic, par=par)
                prepw_war[par] = tq

                def dstf(g0, nb, ic=ic, which=which):
                    return (wgu_s[:, ic * 2048 + which * 1024:
                                  ic * 2048 + which * 1024 + 1024], False)
                prep_transpose_store(N_CC, dstf, tq, par=par)
        for ot in range(8):
            par = kq % 2
            kq += 1
            tw = sch.dma("sync", PREPW[:, par * 2816:par * 2816 + 2816],
                         w_down[ot * 128:(ot + 1) * 128, :], "swq%d" % par,
                         waits=[("ve", prepw_war.get(par, 0))])
            sch.emit("vector", lambda e: e.engine_nop(),
                     waits=[("swq%d" % par, tw)], inc="ve")
            tq = quantize_rows(2816, fold_scale=True, par=par)
            prepw_war[par] = tq

            def dstf(g0, nb, ot=ot):
                return (wd_s.rearrange("p (i o) -> p i o", i=N_IC)[
                    :, g0:g0 + nb, ot * 128:(ot + 1) * 128], True)
            prep_transpose_store(N_IC, dstf, tq, par=par)
        for et in range(24 + 8):
            par = kq % 2
            kq += 1
            if et < 24:
                src = w_attn[et * 128:(et + 1) * 128, :]
            else:
                src = w_proj[(et - 24) * 128:(et - 24 + 1) * 128, :]
            tw = sch.dma("gpsimd", PREPB[:, par * 2816:par * 2816 + 1024],
                         src, "sg_ld%d" % par,
                         waits=[("pe", prepb_war.get(par, 0))])

            def dstf(g0, nb, et=et):
                if et < 16:
                    return (wqk_s[:, et * 1024:(et + 1) * 1024], False)
                if et < 24:
                    vt = et - 16
                    return (wv_s.rearrange("p (c v k) -> p c v k",
                                           c=N_CC, v=8)[:, :, vt], True)
                ot = et - 24
                return (wp_s.rearrange("p (c o k) -> p c o k",
                                       c=N_CC, o=8)[:, :, ot], True)
            prep_transpose_store(N_CC, dstf, tq=sch.val("ve"), par=par,
                                 extra_src_waits=[("sg_ld%d" % par, tw)])

    prep_all()
    dump("wgu", wgu_s, [128, N_IC * 16 * 128], BF16)
    dump("wd", wd_s, [128, N_IC * D], BF16)
    dump("wqk", wqk_s, [128, 16 * D], BF16)
    dump("wv", wv_s, [128, N_CC * D], BF16)
    dump("wp", wp_s, [128, N_CC * D], BF16)
    dump("rsg", RSG[:], [128, N_IC], F32)
    dump("rsu", RSU[:], [128, N_IC], F32)

    if stop_after == "prep":
        return _finish(nc, sch, dbg_outs)

    def ffn(pos):
        if pos == 1:
            phase_barrier()
            ln_stats(lambda t: (XRES[:, t * 1024:(t + 1) * 1024], []),
                     with_max=True)
            tred = ln_finalize(with_max=True)
            tgm = partition_max_to_scalar(RMAX[:], tred)
            tb = global_max(0, tgm)
        else:
            tb = tcc1
        sch.emit("vector", lambda e: e.tensor_tensor(
            out=SCL1[0:1, 1:2], in0=SIGX[0:1, 0:1], in1=GMAXS[0:1, 0:1],
            op=ALU.mult), waits=[("sgc", tb), ("ve", t_sig)],
            inc="ve", small=True)
        sch.emit("vector", lambda e: e.tensor_scalar(
            out=SCL1[0:1, 1:2], in0=SCL1[0:1, 1:2], scalar1=1.0 / QMAX,
            scalar2=None, op0=ALU.mult), inc="ve", small=True)
        t3 = sch.emit("vector", lambda e: e.reciprocal(
            out=SCL1[0:1, 0:1], in_=SCL1[0:1, 1:2]), inc="ve", small=True)
        tbc = bcast_scalars(0, 2, [("ve", t3)])
        sch.emit("vector", lambda e: e.tensor_scalar(
            out=QA[:], in0=RSTD[:], scalar1=SCL[:, 0:1], scalar2=None,
            op0=ALU.mult), waits=[("se", tbc)], inc="ve", small=True)
        sch.emit("vector", lambda e: e.tensor_tensor(
            out=QB[:], in0=MEAN[:], in1=QA[:], op=ALU.mult),
            inc="ve", small=True)
        tqb = sch.emit("vector", lambda e: e.tensor_scalar(
            out=QB[:], in0=QB[:], scalar1=-1.0, scalar2=None, op0=ALU.mult),
            inc="ve", small=True)
        sch.emit("vector", lambda e: e.tensor_scalar(
            out=SGATE[:], in0=RSG[:], scalar1=SCL[:, 1:2], scalar2=None,
            op0=ALU.mult), waits=[("se", tbc)], inc="ve", small=True)
        tsg = sch.emit("vector", lambda e: e.tensor_scalar(
            out=SUP[:], in0=RSU[:], scalar1=SCL[:, 1:2], scalar2=None,
            op0=ALU.mult), inc="ve", small=True)
        tmx = sch.emit("vector", lambda e: e.memset(RMAX[:], 0.0),
                       inc="ve", small=True)

        phase_barrier()
        gev_war = {}
        hev_war = {}
        hsp_war = {}
        wgu_tok = {}
        xint_pe = {}
        xqtm_pe = 0
        n_st = p3_st if p3_st is not None else N_FFN_ST
        n_ic = p3_ic if p3_ic is not None else N_IC
        for st in range(n_st):
            spar = st % 2
            xoff = spar * N_CC * FFN_ST
            t_xint_se = None
            for tt in range(FFN_ST // 128):
                t = st * (FFN_ST // 128) + tt
                if pos == 0:
                    buf = tt % 2
                    xr = XTILE[:, buf * 1024:(buf + 1) * 1024]
                    sem = "sx%d" % buf
                    tload = sch.dma("sync", xr, x_in[t * 128:(t + 1) * 128, :],
                                    sem, waits=[("ve", sch.val("ve"))])
                    wsrc = [(sem, tload)]
                else:
                    xr = XRES[:, t * 1024:(t + 1) * 1024]
                    wsrc = []
                sch.emit("vector", lambda e, xr=xr, t=t: e.tensor_scalar(
                    out=QTMP[:], in0=xr, scalar1=QA[:, t:t + 1],
                    scalar2=QB[:, t:t + 1], op0=ALU.mult, op1=ALU.add),
                    waits=wsrc + [("ve", tqb), ("se", sch.val("se"))], inc="ve")
                sch.emit("vector", lambda e: e.tensor_scalar(
                    out=QTMP[:], in0=QTMP[:], scalar1=MAGIC, scalar2=MAGIC,
                    op0=ALU.add, op1=ALU.subtract), inc="ve")
                tq = sch.emit("vector", lambda e: e.tensor_scalar(
                    out=XQTM[:], in0=QTMP[:], scalar1=QMAX, scalar2=-QMAX - 1.0,
                    op0=ALU.min, op1=ALU.max),
                    waits=[("pe", xqtm_pe)], inc="ve")
                bank = 6 + (tt % 2)
                for b in range(N_CC):
                    xqtm_pe = transpose(
                        PSB[:, bank * 1024 + b * 128:bank * 1024 + (b + 1) * 128],
                        XQTM[:, b * 128:(b + 1) * 128], banks=[bank],
                        waits=[("ve", tq), ("sgi", t_eye_g)])
                dst = XINT[:, xoff:xoff + N_CC * FFN_ST].rearrange(
                    "p (c k) -> p c k", c=N_CC)[:, :, tt * 128:(tt + 1) * 128]
                srcp = PSB[:, bank * 1024:(bank + 1) * 1024].rearrange(
                    "p (c k) -> p c k", c=N_CC)
                t_xint_se = ps_read("scalar",
                                    lambda e, d=dst, s=srcp: e.activation(
                                        out=d, in_=s, func=AF.Copy),
                                    banks=[bank],
                                    waits=[("pe", xint_pe.get(spar, 0))])
            for ic in range(n_ic):
                wbuf = ic % 2
                wsem = "sw%d" % wbuf
                tw = sch.dma("sync", WGUB[:, wbuf * 2048:(wbuf + 1) * 2048],
                             wgu_s[:, ic * 2048:(ic + 1) * 2048], wsem,
                             waits=[("pe", wgu_tok.get(wbuf, 0))])
                par = ic % 2
                gbanks = [0, 1] if par == 0 else [2, 3]
                ubanks = [4, 5] if par == 0 else [6, 7]
                mm_w = [(wsem, tw), ("se", t_xint_se)]
                tok = None
                for base, banks in ((0, gbanks), (1024, ubanks)):
                    for c in range(N_CC):
                        lhs = WGUB[:, wbuf * 2048 + base + c * 128:
                                   wbuf * 2048 + base + (c + 1) * 128]
                        for half in range(2):
                            rhs = XINT[:, xoff + c * FFN_ST + half * 512:
                                       xoff + c * FFN_ST + half * 512 + 512]
                            tok = mm(PS[:, banks[half] * 512:
                                        banks[half] * 512 + 512],
                                     lhs, rhs, banks=[banks[half]],
                                     start=(c == 0), stop=(c == N_CC - 1),
                                     waits=mm_w)
                wgu_tok[wbuf] = tok
                xint_pe[spar] = tok
                gbuf = ic % 2
                hbuf = ic % 2
                gslc = GEV[:, gbuf * 1024:(gbuf + 1) * 1024]
                hslc = HEV[:, hbuf * 1024:(hbuf + 1) * 1024]
                tsilu = ps_read("scalar",
                                lambda e, g=gslc, ic=ic, b=gbanks: e.activation(
                                    out=g,
                                    in_=PS[:, b[0] * 512:b[0] * 512 + 1024],
                                    func=AF.Silu, scale=SGATE[:, ic:ic + 1]),
                                banks=gbanks,
                                waits=[("ve", tsg),
                                       ("ve", gev_war.get(gbuf, 0))])
                tstt = ps_read("vector",
                               lambda e, g=gslc, h=hslc, ic=ic, b=ubanks:
                                   e.scalar_tensor_tensor(
                                       out=h,
                                       in0=PS[:, b[0] * 512:b[0] * 512 + 1024],
                                       scalar=SUP[:, ic:ic + 1], in1=g,
                                       op0=ALU.mult, op1=ALU.mult),
                               banks=ubanks,
                               waits=[("se", tsilu), ("ve", tmx),
                                      ("po", hev_war.get(hbuf, 0))])
                gev_war[gbuf] = tstt
                sch.emit("vector", lambda e, h=hslc: e.tensor_reduce(
                    out=HMC[:], in_=h, axis=AX, op=ALU.max,
                    apply_absolute_value=True), inc="ve", small=True)
                sch.emit("vector", lambda e: e.tensor_tensor(
                    out=RMAX[:], in0=RMAX[:], in1=HMC[:], op=ALU.max),
                    inc="ve", small=True)
                tcast = sch.emit("gpsimd",
                                 lambda e, h=hslc, b=hbuf: e.tensor_copy(
                                     HSP[:, b * 1024:(b + 1) * 1024], h),
                                 waits=[("ve", tstt),
                                        ("sq%d" % hbuf, hsp_war.get(hbuf, 0))],
                                 inc="po")
                hev_war[hbuf] = tcast
                tspill = sch.dma("scalar",
                                 h_s[:, ic * T_CORE + st * FFN_ST:
                                     ic * T_CORE + (st + 1) * FFN_ST],
                                 HSP[:, hbuf * 1024:(hbuf + 1) * 1024],
                                 "sq%d" % hbuf, waits=[("po", tcast)])
                hsp_war[hbuf] = tspill

        if pos == 0:
            dump("h1", h_s, [128, N_IC * T_CORE], BF16)
            dump("s1a", SCL[:, 0:4], [128, 4], F32)
            dump("qa", QA[:], [128, N_TILES], F32)
            dump("qb", QB[:], [128, N_TILES], F32)
        if stop_after == ("ffn%d_p3" % (pos + 1)):
            return None

        tgm2 = partition_max_to_scalar(RMAX[:], sch.val("ve"))
        tb2 = global_max(1, tgm2)
        sch.emit("vector", lambda e: e.tensor_tensor(
            out=SCL1[0:1, 3:4], in0=SIGX[0:1, 1:2], in1=GMAXS[0:1, 1:2],
            op=ALU.mult), waits=[("sgc", tb2)], inc="ve", small=True)
        sch.emit("vector", lambda e: e.tensor_scalar(
            out=SCL1[0:1, 3:4], in0=SCL1[0:1, 3:4], scalar1=1.0 / QMAX,
            scalar2=None, op0=ALU.mult), inc="ve", small=True)
        sch.emit("vector", lambda e: e.reciprocal(
            out=SCL1[0:1, 2:3], in_=SCL1[0:1, 3:4]), inc="ve", small=True)
        t5 = sch.emit("vector", lambda e: e.tensor_scalar(
            out=SCL1[0:1, 3:4], in0=SCL1[0:1, 3:4], scalar1=0.5, scalar2=None,
            op0=ALU.mult), inc="ve", small=True)
        tbc2 = bcast_scalars(2, 2, [("ve", t5)])
        if pos == 0:
            dump("s2a", SCL[:, 0:4], [128, 4], F32)

        phase_barrier()
        twd = sch.dma("sync", WDT[:], wd_s, "swd")
        hq_pe = {}
        hraw_war = {}
        res_tok = {}
        out_toks = []
        h_s3 = h_s.rearrange("p (i t) -> p i t", i=N_IC)
        for st in range(N_P5_ST):
            spar = st % 2
            hoff = spar * N_IC * P5_ST
            tq_ic = {}
            for ic0 in range(0, N_IC, 2):
                nic = min(2, N_IC - ic0)
                w = nic * P5_ST
                buf = (ic0 // 2) % 2
                tl = sch.dma("gpsimd",
                             HRAW[:, buf * 512:buf * 512 + w],
                             h_s3[:, ic0:ic0 + nic,
                                  st * P5_ST:(st + 1) * P5_ST],
                             "sg%d" % buf,
                             waits=[("ve", hraw_war.get(buf, 0))])
                sch.emit("vector", lambda e, b=buf, w=w: e.tensor_scalar(
                    out=P5S[:, 0:w], in0=HRAW[:, b * 512:b * 512 + w],
                    scalar1=SCL[:, 2:3], scalar2=MAGIC, op0=ALU.mult,
                    op1=ALU.add),
                    waits=[("sg%d" % buf, tl), ("se", tbc2),
                           ("pe", hq_pe.get(spar, 0))],
                    inc="ve")
                sch.emit("vector", lambda e, w=w: e.tensor_scalar(
                    out=P5S[:, 0:w], in0=P5S[:, 0:w], scalar1=MAGIC,
                    scalar2=QMAX, op0=ALU.subtract, op1=ALU.min), inc="ve")
                tq = sch.emit("vector", lambda e, ic0=ic0, ho=hoff, w=w:
                              e.tensor_scalar(
                    out=HQ[:, ho + ic0 * P5_ST:ho + ic0 * P5_ST + w],
                    in0=P5S[:, 0:w],
                    scalar1=-QMAX - 1.0, scalar2=None, op0=ALU.max), inc="ve")
                for k in range(nic):
                    tq_ic[ic0 + k] = tq
                hraw_war[buf] = tq
            for tb_i in range(P5_ST // 128):
                t = st * (P5_ST // 128) + tb_i
                pair = t % 2
                banks = [0, 1] if pair == 0 else [2, 3]
                if pos == 0:
                    rsem = "sr%d" % (t % 2)
                    tload = sch.dma("sync", XRES[:, t * 1024:(t + 1) * 1024],
                                    x_in[t * 128:(t + 1) * 128, :], rsem,
                                    waits=[("ve", res_tok.get(t % 2, 0))])
                    res_w = [(rsem, tload)]
                else:
                    res_w = []
                tok = None
                for ic in range(N_IC):
                    lhs = HQ[:, hoff + ic * P5_ST + tb_i * 128:
                             hoff + ic * P5_ST + (tb_i + 1) * 128]
                    for half in range(2):
                        rhs = WDT[:, ic * D + half * 512:
                                  ic * D + half * 512 + 512]
                        tok = mm(PS[:, banks[half] * 512:banks[half] * 512 + 512],
                                 lhs, rhs, banks=[banks[half]],
                                 start=(ic == 0), stop=(ic == N_IC - 1),
                                 waits=[("ve", tq_ic[ic]), ("swd", twd)])
                hq_pe[spar] = tok
                xslc = XRES[:, t * 1024:(t + 1) * 1024]
                tev = ps_read("vector",
                              lambda e, xs=xslc, b=banks:
                                  e.scalar_tensor_tensor(
                                      out=xs,
                                      in0=PS[:, b[0] * 512:b[0] * 512 + 1024],
                                      scalar=SCL[:, 3:4], in1=xs,
                                      op0=ALU.mult, op1=ALU.add),
                              banks=banks, waits=res_w + [("se", tbc2)])
                if pos == 0:
                    res_tok[t % 2] = tev
                if pos == 1:
                    tst = sch.dma("scalar", out[t * 128:(t + 1) * 128, :],
                                  xslc, "so", waits=[("ve", tev)])
                    out_toks.append(tst)
        return out_toks

    ffn(0)
    dump("x1", XRES[:], [128, N_TILES * 1024], F32)
    if stop_after in ("ffn1_p3", "ffn1"):
        return _finish(nc, sch, dbg_outs)

    def attention():
        phase_barrier()
        ln_stats(lambda t: (XRES[:, t * 1024:(t + 1) * 1024], []),
                 with_max=False)
        ln_finalize(with_max=False)
        sch.emit("vector", lambda e: e.tensor_tensor(
            out=QB[:], in0=MEAN[:], in1=RSTD[:], op=ALU.mult),
            waits=[("se", sch.val("se"))], inc="ve", small=True)
        tnm = sch.emit("vector", lambda e: e.tensor_scalar(
            out=QB[:], in0=QB[:], scalar1=-1.0, scalar2=None, op0=ALU.mult),
            inc="ve", small=True)

        ysb_pe = 0
        xlnf_pe = 0
        xln_pe = 0
        wqk_tok = {}
        wv_tok = {}
        wp_tok = {}
        norm_toks = {}
        n_ast = att_st if att_st is not None else N_ATT_ST

        def emit_ty(r, qt, g, tnrm, ebuf):
            nonlocal ysb_pe
            wloc = qt // 2
            for h8 in range(8):
                for jc in range(2):
                    off = h8 * 256 + jc * 128
                    transpose(
                        PSB[:, 4 * 1024 + off:4 * 1024 + off + 128],
                        EXPS[:, ebuf + off:ebuf + off + 128],
                        banks=[4 + off // 1024],
                        waits=[("ve", tnrm), ("sgi", t_eye_g)])
            tevt = ps_read("scalar", lambda e: e.activation(
                out=ATTT[:], in_=PSB[:, 4096:6144], func=AF.Copy),
                banks=[4, 5])
            tp = (0, 64) if g else None
            for h8 in range(8):
                h = 2 * h8 + g
                for jc in range(2):
                    jb = wloc * 2 + jc
                    lhs = VS[:, jb * D + h * 64:jb * D + (h + 1) * 64]
                    rhs = ATTT[:, h8 * 256 + jc * 128:
                               h8 * 256 + (jc + 1) * 128]
                    oslc = PS[:, 3072 + h8 * 128:3072 + (h8 + 1) * 128]
                    oslc = (oslc[64:128, :] if g else oslc[0:64, :])
                    mm(oslc, lhs, rhs, banks=[6 + h8 // 4],
                       start=(jc == 0), stop=(jc == 1),
                       waits=[("se", tevt)], tile_position=tp)
            dsty = YSB.rearrange("p (pr k) -> p pr k", pr=NH // 2)[
                g * 64:g * 64 + 64, :, qt * 128:(qt + 1) * 128]
            srcy = PS[g * 64:g * 64 + 64, 3072:4096].rearrange(
                "p (pr k) -> p pr k", pr=8)
            ps_read("scalar", lambda e, d=dsty, s=srcy: e.activation(
                out=d, in_=s, func=AF.Copy),
                banks=[6, 7], waits=[("pe", ysb_pe)])

        t_x_se_par = {}

        def emit_ln(st2):
            nonlocal xln_pe, xlnf_pe
            lpar = st2 % 2
            loff = lpar * N_CC * ATT_ST
            t_x_se = None
            for tt in range(ATT_ST // 128):
                t = st2 * (ATT_ST // 128) + tt
                xs = XRES[:, t * 1024:(t + 1) * 1024]
                tln = sch.emit("vector", lambda e, xs=xs, t=t: e.tensor_scalar(
                    out=XLN[:], in0=xs, scalar1=RSTD[:, t:t + 1],
                    scalar2=QB[:, t:t + 1], op0=ALU.mult, op1=ALU.add),
                    waits=[("ve", tnm), ("se", sch.val("se")),
                           ("pe", xln_pe)], inc="ve")
                bank = 6 + (tt % 2)
                for b in range(N_CC):
                    xln_pe = transpose(
                        PSB[:, bank * 1024 + b * 128:bank * 1024 + (b + 1) * 128],
                        XLN[:, b * 128:(b + 1) * 128], banks=[bank],
                        waits=[("ve", tln), ("sgi", t_eye_g)])
                dst = XLNF[:, loff:loff + N_CC * ATT_ST].rearrange(
                    "p (c k) -> p c k", c=N_CC)[:, :, tt * 128:(tt + 1) * 128]
                srcp = PSB[:, bank * 1024:(bank + 1) * 1024].rearrange(
                    "p (c k) -> p c k", c=N_CC)
                t_x_se = ps_read("scalar",
                                 lambda e, d=dst, s=srcp: e.activation(
                                     out=d, in_=s, func=AF.Copy),
                                 banks=[bank], waits=[("pe", xlnf_pe)])
            t_x_se_par[lpar] = t_x_se

        emit_ln(0)
        for st in range(n_ast):
            lpar = st % 2
            loff = lpar * N_CC * ATT_ST
            XLNF_ST = XLNF[:, loff:loff + N_CC * ATT_ST]
            t_x_se = t_x_se_par[lpar]
            for grp4 in range(4):
                for j in range(4):
                    qkc = grp4 * 4 + j
                    wbuf = qkc % 2
                    wsem = "sw%d" % wbuf
                    tw = sch.dma("sync", WQKB[:, wbuf * 1024:(wbuf + 1) * 1024],
                                 wqk_s[:, qkc * 1024:(qkc + 1) * 1024], wsem,
                                 waits=[("pe", wqk_tok.get(wbuf, 0))])
                    tok = None
                    for c in range(N_CC):
                        lhs = WQKB[:, wbuf * 1024 + c * 128:
                                   wbuf * 1024 + (c + 1) * 128]
                        rhs = XLNF_ST[:, c * ATT_ST:(c + 1) * ATT_ST]
                        tok = mm(PS[:, j * 512:j * 512 + 512], lhs, rhs,
                                 banks=[j], start=(c == 0),
                                 stop=(c == N_CC - 1),
                                 waits=[(wsem, tw), ("se", t_x_se)])
                    wqk_tok[wbuf] = tok
                    xlnf_pe = max(xlnf_pe, tok)
                scale = 0.125 if grp4 < 2 else 1.0
                ps_read("scalar", lambda e, g4=grp4, s=scale: e.activation(
                    out=QKS[:, g4 * 2048:(g4 + 1) * 2048],
                    in_=PS[:, 0:2048], func=AF.Copy, scale=np.float32(s)),
                    banks=[0, 1, 2, 3])
            for c in range(N_CC):
                wbuf = c % 2
                vsem = "sv%d" % wbuf
                tw = sch.dma("sync", WVB[:, wbuf * 1024:(wbuf + 1) * 1024],
                             wv_s[:, c * D:(c + 1) * D], vsem,
                             waits=[("pe", wv_tok.get(wbuf, 0))])
                tok = None
                for tb_i in range(4):
                    lhs = XLNF_ST[:, c * ATT_ST + tb_i * 128:
                                  c * ATT_ST + (tb_i + 1) * 128]
                    for half in range(2):
                        bank = tb_i * 2 + half
                        rhs = WVB[:, wbuf * 1024 + half * 512:
                                  wbuf * 1024 + (half + 1) * 512]
                        tok = mm(PS[:, bank * 512:bank * 512 + 512], lhs, rhs,
                                 banks=[bank], start=(c == 0),
                                 stop=(c == N_CC - 1),
                                 waits=[(vsem, tw), ("se", t_x_se)])
                wv_tok[wbuf] = tok
            for tb_i in range(4):
                ps_read("scalar", lambda e, tb_i=tb_i: e.activation(
                    out=VS[:, tb_i * D:(tb_i + 1) * D],
                    in_=PS[:, tb_i * 1024:(tb_i + 1) * 1024], func=AF.Copy),
                    banks=[tb_i * 2, tb_i * 2 + 1])
            if st + 1 < n_ast:
                emit_ln(st + 1)
            rounds = [(qt, g) for qt in range(4) for g in range(2)]
            prev = None
            for r0, (qt, g) in enumerate(rounds):
                r = st * 8 + r0
                wloc = qt // 2
                ebuf = (r % 2) * 2048
                tnrm = None
                for hf in range(2):
                    for h8 in range(hf * 4, hf * 4 + 4):
                        qc = h8
                        lhs = QKS[:, qc * ATT_ST + qt * 128:
                                  qc * ATT_ST + (qt + 1) * 128][
                            g * 64:g * 64 + 64, :]
                        rhs = QKS[:, (8 + qc) * ATT_ST + wloc * WIN:
                                  (8 + qc) * ATT_ST + (wloc + 1) * WIN][
                            g * 64:g * 64 + 64, :]
                        mm(PS[:, h8 * 256:h8 * 256 + 256], lhs, rhs,
                           banks=[h8 // 2], start=True, stop=True)
                    hb = [2 * hf, 2 * hf + 1]
                    eoff = ebuf + hf * 1024
                    texp = ps_read("scalar", lambda e, eo=eoff, hf=hf:
                                   e.activation(
                        out=EXPS[:, eo:eo + 1024],
                        in_=PS[:, hf * 1024:hf * 1024 + 1024], func=AF.Exp,
                        bias=np.float32(0.0), scale=np.float32(1.0)),
                        banks=hb,
                        waits=[("ve", norm_toks.get((r - 2, hf), 0))])
                    sch.emit("vector", lambda e, eo=eoff, hf=hf: e.tensor_reduce(
                        out=SUMS[:, hf * 4:hf * 4 + 4],
                        in_=EXPS[:, eo:eo + 1024].rearrange(
                            "p (h j) -> p h j", h=4), axis=AX, op=ALU.add),
                        waits=[("se", texp)], inc="ve", small=True)
                    trec = sch.emit("vector", lambda e, hf=hf: e.reciprocal(
                        out=RECS[:, hf * 4:hf * 4 + 4],
                        in_=SUMS[:, hf * 4:hf * 4 + 4]), inc="ve", small=True)
                    tnrm = sch.emit("vector", lambda e, eo=eoff, hf=hf:
                                    e.tensor_tensor(
                        out=EXPS[:, eo:eo + 1024].rearrange(
                            "p (h j) -> p h j", h=4),
                        in0=EXPS[:, eo:eo + 1024].rearrange(
                            "p (h j) -> p h j", h=4),
                        in1=RECS[:, hf * 4:hf * 4 + 4].rearrange(
                            "p (h o) -> p h o", o=1)
                            .broadcast_to([128, 4, WIN]),
                        op=ALU.mult), waits=[("ve", trec)], inc="ve")
                    norm_toks[(r, hf)] = tnrm
                if prev is not None:
                    emit_ty(*prev)
                prev = (r, qt, g, tnrm, ebuf)
            emit_ty(*prev)
            t_y_se = sch.val("se")
            tokp = None
            for p in range(NH // 2):
                wbuf = p % 2
                psem = "sv%d" % wbuf
                tw = sch.dma("sync", WPB[:, wbuf * 1024:(wbuf + 1) * 1024],
                             wp_s[:, p * D:(p + 1) * D], psem,
                             waits=[("pe", wv_tok.get(wbuf, 0))])
                for tb_i in range(4):
                    lhs = YSB[:, p * ATT_ST + tb_i * 128:
                              p * ATT_ST + (tb_i + 1) * 128]
                    for half in range(2):
                        bank = tb_i * 2 + half
                        tokp = mm(PS[:, bank * 512:bank * 512 + 512], lhs,
                                  WPB[:, wbuf * 1024 + half * 512:
                                      wbuf * 1024 + (half + 1) * 512],
                                  banks=[bank], start=(p == 0),
                                  stop=(p == NH // 2 - 1),
                                  waits=[(psem, tw), ("se", t_y_se)])
                wv_tok[wbuf] = tokp
            ysb_pe = tokp
            for tb_i in range(4):
                t = st * 4 + tb_i
                banks = [tb_i * 2, tb_i * 2 + 1]
                xslc = XRES[:, t * 1024:(t + 1) * 1024]
                ps_read("vector", lambda e, xs=xslc, b=banks: e.tensor_tensor(
                    out=xs, in0=PS[:, b[0] * 512:b[0] * 512 + 1024], in1=xs,
                    op=ALU.add), banks=banks)

    attention()
    dump("x2", XRES[:], [128, N_TILES * 1024], F32)
    if stop_after == "attn":
        return _finish(nc, sch, dbg_outs)

    ffn(1)
    return _finish(nc, sch, dbg_outs)


def _finish(nc, sch, dbg_outs):
    final = sch.val("so")
    if final > 0:
        sch.emit("sync", lambda e: e.nop(), waits=[("so", final)])
    sch.build()
    return nc, dbg_outs


def _in_maps(inputs):
    x = np.ascontiguousarray(
        np.asarray(inputs["L0_attn_x"], dtype=np.float32).reshape(-1, D))
    shards = [x[r * T_CORE:(r + 1) * T_CORE] for r in range(N_CORES)]
    eye_f32 = np.eye(128, dtype=np.float32)
    base = dict(
        w_attn=np.ascontiguousarray(np.asarray(inputs["L0_attn_w_attn"], np.float32)),
        w_proj=np.ascontiguousarray(np.asarray(inputs["L0_attn_w_proj"], np.float32)),
        w_gate=np.ascontiguousarray(np.asarray(inputs["L0_ffn_w_gate"], np.float32)),
        w_up=np.ascontiguousarray(np.asarray(inputs["L0_ffn_w_up"], np.float32)),
        w_down=np.ascontiguousarray(np.asarray(inputs["L0_ffn_w_down"], np.float32)),
        clip_x=np.asarray(inputs["L0_ffn_clip_x"], np.float32).reshape(1, 1),
        clip_down=np.asarray(inputs["L0_ffn_clip_down"], np.float32).reshape(1, 1),
        eye_f32=eye_f32,
    )
    return [dict(base, x=shards[r]) for r in range(N_CORES)]


def kernel(**inputs):
    nc, _ = build_program()
    in_maps = _in_maps(inputs)
    res = run_bass_kernel_spmd(nc, in_maps, core_ids=list(range(N_CORES)))
    parts = [res.results[r]["out"] for r in range(N_CORES)]
    return np.concatenate(parts, axis=0).reshape(4, 8192, D).astype(np.float32)


# version 13 (final)
# speedup vs baseline: 1.0825x; 1.0449x over previous
from contextlib import ExitStack

import numpy as np
import ml_dtypes

import concourse.bass as bass
import concourse.mybir as mybir
from concourse.bass_utils import run_bass_kernel_spmd

F32 = mybir.dt.float32
BF16 = mybir.dt.bfloat16
I32 = mybir.dt.int32
AF = mybir.ActivationFunctionType
ALU = mybir.AluOpType
AX = mybir.AxisListType.X

MAGIC = float(np.float32(1.5 * 2 ** 23))
QMAX = 127.0

N_CORES = 8
T_CORE = 4096
D = 1024
I_FFN = 2816
NH = 16
HD = 64
WIN = 256
N_TILES = T_CORE // 128
N_IC = I_FFN // 128
N_CC = D // 128
FFN_ST = 1024
N_FFN_ST = T_CORE // FFN_ST
P5_ST = 256
N_P5_ST = T_CORE // P5_ST
ATT_ST = 512
N_ATT_ST = T_CORE // ATT_ST
EPS = 1e-5

ALL_SEMS = ("pe", "ve", "se", "po", "sx", "sw", "sg", "sq", "cc")


class Sched:

    ENGINES = ("tensor", "vector", "scalar", "gpsimd", "sync")

    def __init__(self, nc):
        self.nc = nc
        self.ops = {e: [] for e in self.ENGINES}
        self.es = ExitStack()
        self.sems = {}
        self.counts = {}
        self.waited = {}
        self.ps_w = [0] * 8
        self.ps_r = [dict() for _ in range(8)]
        self.pending_small = {}
        self.eng_sem = {"tensor": "pe", "vector": "ve", "scalar": "se",
                        "gpsimd": "po"}

    def sem(self, name):
        if name not in self.sems:
            self.sems[name] = self.es.enter_context(self.nc.semaphore(name))
            self.counts[name] = 0
        return self.sems[name]

    def emit(self, engine, fn, waits=(), inc=None, inc_amt=1, small=False):
        if inc:
            self.sem(inc)
        ws = []
        if engine in ("vector", "scalar", "gpsimd") and \
                inc == self.eng_sem.get(engine) and self.counts.get(inc, 0) > 0:
            ws.append((inc, self.counts[inc]))
        ps = self.pending_small.pop(engine, None)
        if ps is not None:
            ws.append(ps)
            self.sem(ps[0])
        for s, v in waits:
            self.sem(s)
            assert v <= self.counts[s], f"wait on future token {s}:{v}"
            if v > 0:
                ws.append((s, v))
        tok = None
        if inc is not None:
            self.counts[inc] += inc_amt
            tok = self.counts[inc]
        if small:
            assert inc == self.eng_sem.get(engine), \
                f"small op on {engine} must inc {self.eng_sem.get(engine)}"
            self.pending_small[engine] = (inc, tok)
        self.ops[engine].append((fn, tuple(ws), inc, inc_amt))
        return tok

    def val(self, semname):
        return self.counts.get(semname, 0)

    def dma(self, queue, out, in_, sem, waits=()):
        return self.emit(queue, lambda e, o=out, i=in_: e.dma_start(out=o, in_=i),
                         waits=waits, inc=sem, inc_amt=16)

    def ps_write_waits(self, banks):
        ws = []
        for b in banks:
            for s, v in self.ps_r[b].items():
                ws.append((s, v))
        return ws

    def ps_mark_write(self, banks, pe_tok):
        for b in banks:
            self.ps_w[b] = pe_tok
            self.ps_r[b] = {}

    def ps_read_waits(self, banks):
        return [("pe", max(self.ps_w[b] for b in banks))]

    def ps_mark_read(self, banks, sem, tok):
        for b in banks:
            self.ps_r[b][sem] = max(self.ps_r[b].get(sem, 0), tok)

    COMPUTE_SEMS = ("pe", "ve", "se", "po")

    def _coalesce_incs(self):
        waited_vals = {}
        for e in self.ENGINES:
            for _, waits, _, _ in self.ops[e]:
                for s, v in waits:
                    waited_vals.setdefault(s, set()).add(v)
        for e in self.ENGINES:
            newops = []
            running = {}
            pending = {}
            last_inc_idx = {}
            for fn, waits, inc, inc_amt in self.ops[e]:
                newops.append([fn, waits, None, 0])
                if inc is None or inc not in self.COMPUTE_SEMS:
                    if inc is not None:
                        newops[-1][2] = inc
                        newops[-1][3] = inc_amt
                    continue
                running[inc] = running.get(inc, 0) + inc_amt
                pending[inc] = pending.get(inc, 0) + inc_amt
                last_inc_idx[inc] = len(newops) - 1
                if running[inc] in waited_vals.get(inc, ()):
                    newops[-1][2] = inc
                    newops[-1][3] = pending[inc]
                    pending[inc] = 0
            for s, amt in pending.items():
                if amt:
                    idx = last_inc_idx[s]
                    if newops[idx][2] is None:
                        newops[idx][2] = s
                        newops[idx][3] = amt
                    else:
                        assert newops[idx][2] == s
                        newops[idx][3] += amt
            self.ops[e] = [tuple(t) for t in newops]

    def replay(self, engine, eng):
        for fn, waits, inc, inc_amt in self.ops[engine]:
            for s, v in waits:
                key = (engine, s)
                if v <= self.waited.get(key, 0):
                    continue
                eng.wait_ge(self.sems[s], v)
                self.waited[key] = v
            inst = fn(eng)
            if inc is not None and inc_amt > 0:
                assert inst is not None
                inst.then_inc(self.sems[inc], inc_amt)

    def build(self):
        self._coalesce_incs()
        with self.nc.Block() as blk:
            blk.tensor(lambda eng: self.replay("tensor", eng))
            blk.vector(lambda eng: self.replay("vector", eng))
            blk.scalar(lambda eng: self.replay("scalar", eng))
            blk.gpsimd(lambda eng: self.replay("gpsimd", eng))
            blk.sync(lambda eng: self.replay("sync", eng))
        self.es.close()


def build_program(dbg=(), stop_after=None, p3_st=None, p3_ic=None, att_st=None, att_rounds=True, att_lvl=3):
    dbg = set(dbg)
    nc = bass.Bass("TRN2", target_bir_lowering=False)

    x_in = nc.dram_tensor("x", [T_CORE, D], F32, kind="ExternalInput").ap()
    w_attn = nc.dram_tensor("w_attn", [3 * D, D], F32, kind="ExternalInput").ap()
    w_proj = nc.dram_tensor("w_proj", [D, D], F32, kind="ExternalInput").ap()
    w_gate = nc.dram_tensor("w_gate", [I_FFN, D], F32, kind="ExternalInput").ap()
    w_up = nc.dram_tensor("w_up", [I_FFN, D], F32, kind="ExternalInput").ap()
    w_down = nc.dram_tensor("w_down", [D, I_FFN], F32, kind="ExternalInput").ap()
    clip_x = nc.dram_tensor("clip_x", [1, 1], F32, kind="ExternalInput").ap()
    clip_down = nc.dram_tensor("clip_down", [1, 1], F32, kind="ExternalInput").ap()
    eyef_in = nc.dram_tensor("eye_f32", [128, 128], F32, kind="ExternalInput").ap()
    out = nc.dram_tensor("out", [T_CORE, D], F32, kind="ExternalOutput").ap()

    wgu_s = nc.dram_tensor("wgu_s", [128, N_IC * 16 * 128], BF16, kind="Internal").ap()
    wd_s = nc.dram_tensor("wd_s", [128, N_IC * D], BF16, kind="Internal").ap()
    wqk_s = nc.dram_tensor("wqk_s", [128, 16 * D], BF16, kind="Internal").ap()
    wv_s = nc.dram_tensor("wv_s", [128, N_CC * D], BF16, kind="Internal").ap()
    wp_s = nc.dram_tensor("wp_s", [128, N_CC * D], BF16, kind="Internal").ap()
    h_s = nc.dram_tensor("h_s", [128, N_IC * T_CORE], BF16, kind="Internal").ap()
    cc_in = nc.dram_tensor("cc_in", [1, 1], F32, kind="Internal").ap()
    cc_out = nc.dram_tensor("cc_out", [1, 1], F32, kind="Internal").ap()

    sch = Sched(nc)
    es = sch.es

    def sb(name, shape, dtype):
        return nc.alloc_sbuf_tensor(name, shape, dtype)

    XRES = sb("XRES", [128, N_TILES * 1024], F32)
    EYEB = sb("EYEB", [128, 128], BF16)
    EYEF = sb("EYEF", [128, 128], F32)
    ONES = sb("ONES", [1, 128], F32)
    SUM = sb("SUM", [128, N_TILES], F32)
    SSQ = sb("SSQ", [128, N_TILES], F32)
    MX = sb("MX", [128, N_TILES], F32)
    MN = sb("MN", [128, N_TILES], F32)
    MEAN = sb("MEAN", [128, N_TILES], F32)
    RSTD = sb("RSTD", [128, N_TILES], F32)
    QA = sb("QA", [128, N_TILES], F32)
    QB = sb("QB", [128, N_TILES], F32)
    T0 = sb("T0", [128, N_TILES], F32)
    T1 = sb("T1", [128, N_TILES], F32)
    RMAX = sb("RMAX", [128, 1], F32)
    HMC = sb("HMC", [128, 1], F32)
    SCL = sb("SCL", [128, 8], F32)
    SCL1 = sb("SCL1", [1, 8], F32)
    SGATE = sb("SGATE", [128, N_IC], F32)
    SUP = sb("SUP", [128, N_IC], F32)
    RSG = sb("RSG", [128, N_IC], F32)
    RSU = sb("RSU", [128, N_IC], F32)
    SIGX = sb("SIGX", [1, 2], F32)
    GMAXS = sb("GMAXS", [1, 2], F32)
    NEGM = sb("NEGM", [128, 1], F32)
    SUMS = sb("SUMS", [128, 8], F32)
    RECS = sb("RECS", [128, 8], F32)
    QTMP = sb("QTMP", [128, 1024], F32)

    ARENA_BYTES = 74240
    ARENA = nc.alloc_sbuf_tensor("ARENA", [128, ARENA_BYTES], mybir.dt.uint8)
    abase = nc.lookup_mloc(ARENA).addr

    def at(name, shape, dtype, off):
        nbytes = int(np.prod(shape[1:])) * mybir.dt.size(dtype)
        assert off + nbytes <= ARENA_BYTES, f"{name} overflows arena"
        return nc.alloc_sbuf_tensor_at(name, shape, dtype, offset=abase + off)

    PREPW = at("PREPW", [128, 2 * 2816], F32, 0)
    PREPS = at("PREPS", [128, 2816], F32, 22528)
    PREPB = at("PREPB", [128, 2 * 2816], BF16, 33792)
    PREPO = at("PREPO", [128, 2 * 2816], BF16, 45056)
    PREPR = at("PREPR", [128, 1], F32, 56320)
    P1X = at("P1X", [128, 2 * 1024], F32, 59392)
    XTILE = at("XTILE", [128, 2 * 1024], F32, 0)
    XQTM = at("XQTM", [128, 1024], BF16, 8192)
    XINT = at("XINT", [128, 2 * N_CC * FFN_ST], BF16, 10240)
    WGUB = at("WGUB", [128, 2 * 2048], BF16, 43008)
    GEV = at("GEV", [128, 2 * 1024], F32, 51200)
    HEV = at("HEV", [128, 2 * 1024], F32, 59392)
    HSP = at("HSP", [128, 2 * 1024], BF16, 67584)
    WDT = at("WDT", [128, N_IC * D], BF16, 0)
    HRAW = at("HRAW", [128, 2 * 2 * P5_ST], BF16, 45056)
    HQ = at("HQ", [128, 2 * N_IC * P5_ST], BF16, 47104)
    P5S = at("P5S", [128, 2 * P5_ST], F32, 69632)
    QKS = at("QKS", [128, 16 * ATT_ST], BF16, 0)
    VS = at("VS", [128, (ATT_ST // 128) * D], BF16, 16384)
    XLNF = at("XLNF", [128, 2 * N_CC * ATT_ST], BF16, 24576)
    XLN = at("XLN", [128, 1024], BF16, 40960)
    WQKB = at("WQKB", [128, 2 * 1024], BF16, 43008)
    WVB = at("WVB", [128, 2 * 1024], BF16, 47104)
    WPB = WVB
    EXPS = at("EXPS", [128, 2 * 8 * WIN], BF16, 51200)
    ATTT = at("ATTT", [128, 8 * WIN], BF16, 59392)
    YSB = at("YSB", [128, (NH // 2) * ATT_ST], BF16, 63488)

    PSH = nc.alloc_psum_tensor("PS", [128, 4096], F32)
    PS = PSH[:]
    PSB = PSH.bitcast(BF16)[:]
    PSI = PSH.bitcast(I32)[:]
    ATTT_I = ATTT.bitcast(I32)

    dbg_outs = {}

    def barrier_waits():
        return [(s, sch.val(s)) for s in list(sch.counts.keys())]

    def phase_barrier():
        w = barrier_waits()
        for eng in Sched.ENGINES:
            sch.emit(eng, lambda e: e.nop(), waits=w)

    def dump(name, src, shape, dtype=F32):
        if name not in dbg:
            return
        d = nc.dram_tensor("dbg_" + name, list(shape), dtype,
                           kind="ExternalOutput").ap()
        dbg_outs[name] = d
        sch.dma("sync", d, src, "so", waits=barrier_waits())

    def mm(out_ap, lhsT, rhs, banks, start, stop, waits=(), tile_position=None):
        ws = list(waits) + (sch.ps_write_waits(banks) if start else [])
        tok = sch.emit(
            "tensor",
            lambda e, o=out_ap, l=lhsT, r=rhs, s0=start, s1=stop, tp=tile_position:
                e.matmul(o, l, r, start=s0, stop=s1, tile_position=tp),
            waits=ws, inc="pe")
        sch.ps_mark_write(banks, tok)
        return tok

    def transpose(out_ap, in_ap, banks, waits=(), identity=None):
        idn = identity if identity is not None else EYEB[:]
        ws = list(waits) + sch.ps_write_waits(banks)
        tok = sch.emit(
            "tensor",
            lambda e, o=out_ap, i=in_ap, idn=idn: e.transpose(o, i, idn),
            waits=ws, inc="pe")
        sch.ps_mark_write(banks, tok)
        return tok

    def ps_read(engine, fn, banks, waits=(), small=False):
        sem = {"scalar": "se", "vector": "ve", "gpsimd": "po"}[engine]
        ws = list(waits) + sch.ps_read_waits(banks)
        tok = sch.emit(engine, fn, waits=ws, inc=sem, small=small)
        sch.ps_mark_read(banks, sem, tok)
        return tok

    sch.dma("gpsimd", EYEB[:], eyef_in, "sgi")
    sch.dma("sync", EYEF[:], eyef_in, "si")
    sch.dma("sync", SIGX[0:1, 0:1], clip_x, "si")
    sch.dma("sync", SIGX[0:1, 1:2], clip_down, "si")
    t_eye_g = sch.val("sgi")
    t_eye = sch.val("si")
    sch.emit("vector", lambda e: e.memset(ONES[:], 1.0), inc="ve")
    t = sch.val("si")
    sch.emit("scalar", lambda e: e.activation(out=SCL1[0:1, 0:2],
                                              in_=SIGX[0:1, 0:2],
                                              func=AF.Exp, scale=np.float32(-1.0)),
             waits=[("si", t)], inc="se", small=True)
    sch.emit("vector", lambda e: e.tensor_scalar(out=SCL1[0:1, 0:2],
                                                 in0=SCL1[0:1, 0:2], scalar1=1.0,
                                                 scalar2=None, op0=ALU.add),
             waits=[("se", sch.val("se"))], inc="ve", small=True)
    t_sig = sch.emit("vector", lambda e: e.reciprocal(out=SIGX[0:1, 0:2],
                                                      in_=SCL1[0:1, 0:2]),
                     inc="ve", small=True)

    def ln_stats(tile_src_fn, with_max):
        for t in range(N_TILES):
            src, waits = tile_src_fn(t)
            sch.emit("scalar",
                     lambda e, s=src, t=t: e.activation(
                         out=QTMP[:], in_=s, func=AF.Square,
                         accum_out=SSQ[:, t:t + 1]),
                     waits=waits, inc="se", small=True)
            sch.emit("vector",
                     lambda e, s=src, t=t: e.tensor_reduce(
                         out=SUM[:, t:t + 1], in_=s, axis=AX, op=ALU.add),
                     waits=waits, inc="ve", small=True)
            if with_max:
                sch.emit("vector",
                         lambda e, s=src, t=t: e.tensor_reduce(
                             out=MX[:, t:t + 1], in_=s, axis=AX, op=ALU.max),
                         waits=waits, inc="ve", small=True)
                sch.emit("vector",
                         lambda e, s=src, t=t: e.tensor_reduce(
                             out=MN[:, t:t + 1], in_=s, axis=AX, op=ALU.min),
                         waits=waits, inc="ve", small=True)

    def ln_finalize(with_max):
        sl = slice(0, N_TILES)
        w = [("se", sch.val("se")), ("ve", sch.val("ve")), ("po", sch.val("po"))]
        sch.emit("vector", lambda e: e.tensor_scalar(
            out=MEAN[:, sl], in0=SUM[:, sl], scalar1=1.0 / D, scalar2=None,
            op0=ALU.mult), waits=w, inc="ve", small=True)
        sch.emit("vector", lambda e: e.tensor_scalar(
            out=T0[:, sl], in0=SSQ[:, sl], scalar1=1.0 / D, scalar2=None,
            op0=ALU.mult), inc="ve", small=True)
        sch.emit("vector", lambda e: e.tensor_tensor(
            out=T1[:, sl], in0=MEAN[:, sl], in1=MEAN[:, sl], op=ALU.mult),
            inc="ve", small=True)
        tvar = sch.emit("vector", lambda e: e.tensor_tensor(
            out=T0[:, sl], in0=T0[:, sl], in1=T1[:, sl], op=ALU.subtract),
            inc="ve", small=True)
        sch.emit("scalar", lambda e: e.activation(
            out=T1[:, sl], in_=T0[:, sl], func=AF.Ln, bias=np.float32(EPS)),
            waits=[("ve", tvar)], inc="se", small=True)
        trstd = sch.emit("scalar", lambda e: e.activation(
            out=RSTD[:, sl], in_=T1[:, sl], func=AF.Exp, scale=np.float32(-0.5)),
            inc="se", small=True)
        if not with_max:
            return trstd
        sch.emit("vector", lambda e: e.tensor_tensor(
            out=T0[:, sl], in0=MX[:, sl], in1=MEAN[:, sl], op=ALU.subtract),
            waits=[("se", trstd)], inc="ve", small=True)
        sch.emit("vector", lambda e: e.tensor_tensor(
            out=T1[:, sl], in0=MEAN[:, sl], in1=MN[:, sl], op=ALU.subtract),
            inc="ve", small=True)
        sch.emit("vector", lambda e: e.tensor_tensor(
            out=T0[:, sl], in0=T0[:, sl], in1=T1[:, sl], op=ALU.max),
            inc="ve", small=True)
        sch.emit("vector", lambda e: e.tensor_tensor(
            out=T0[:, sl], in0=T0[:, sl], in1=RSTD[:, sl], op=ALU.mult),
            inc="ve", small=True)
        tred = sch.emit("vector", lambda e: e.tensor_reduce(
            out=RMAX[:], in_=T0[:, sl], axis=AX, op=ALU.max),
            inc="ve", small=True)
        return tred

    def partition_max_to_scalar(src_col, tred):
        transpose(PS[0:1, 3584:3712], src_col, banks=[7],
                  waits=[("ve", tred), ("si", t_eye)], identity=EYEF[:])
        tgm = ps_read("vector", lambda e: e.tensor_reduce(
            out=T0[0:1, 0:1], in_=PS[0:1, 3584:3712], axis=AX, op=ALU.max),
            banks=[7], small=True)
        return tgm

    def global_max(idx, tgm):
        ts = sch.dma("gpsimd", cc_in, T0[0:1, 0:1], "sgc",
                     waits=[("ve", tgm), ("cc", sch.val("cc"))])
        tcc = sch.emit("gpsimd",
                       lambda e: e.collective_compute(
                           "AllReduce", ALU.max,
                           replica_groups=[list(range(N_CORES))],
                           ins=[cc_in.opt()], outs=[cc_out.opt()]),
                       waits=[("sgc", ts)], inc="cc")
        tb = sch.dma("gpsimd", GMAXS[0:1, idx:idx + 1], cc_out, "sgc",
                     waits=[("cc", tcc)])
        return tb

    def bcast_scalars(col0, n, waits):
        mm(PS[:, 3584:3584 + n], ONES[:], SCL1[0:1, col0:col0 + n],
           banks=[7], start=True, stop=True,
           waits=list(waits) + [("ve", sch.val("ve")), ("si", t_eye)])
        return ps_read("scalar", lambda e: e.activation(
            out=SCL[:, col0:col0 + n], in_=PS[:, 3584:3584 + n],
            func=AF.Copy), banks=[7], small=True)

    def p1_src_hbm(t):
        buf = t % 2
        dst = P1X[:, buf * 1024:(buf + 1) * 1024]
        ws = [("se", sch.val("se")), ("ve", sch.val("ve")), ("po", sch.val("po"))]
        sem = "sx%d" % buf
        tload = sch.dma("sync", dst, x_in[t * 128:(t + 1) * 128, :], sem,
                        waits=ws)
        return dst, [(sem, tload)]

    ln_stats(p1_src_hbm, with_max=True)
    tred = ln_finalize(with_max=True)
    tgm = partition_max_to_scalar(RMAX[:], tred)
    tcc1 = global_max(0, tgm)

    def quantize_rows(n, fold_scale, rs_dst=None, rs_col=None, par=0):
        pw = PREPW[:, par * 2816:par * 2816 + n]
        pb = PREPB[:, par * 2816:par * 2816 + n]
        sch.emit("vector", lambda e, pw=pw: e.tensor_reduce(
            out=PREPR[:], in_=pw, axis=AX, op=ALU.max,
            apply_absolute_value=True), inc="ve", small=True)
        if rs_dst is not None:
            sch.emit("vector", lambda e: e.tensor_scalar(
                out=rs_dst[:, rs_col:rs_col + 1], in0=PREPR[:],
                scalar1=1.0 / QMAX, scalar2=None, op0=ALU.mult),
                inc="ve", small=True)
        sch.emit("vector", lambda e: e.reciprocal(out=HMC[:], in_=PREPR[:]),
                 inc="ve", small=True)
        sch.emit("vector", lambda e: e.tensor_scalar(
            out=HMC[:], in0=HMC[:], scalar1=QMAX, scalar2=None, op0=ALU.mult),
            inc="ve", small=True)
        sch.emit("vector", lambda e, pw=pw: e.tensor_scalar(
            out=PREPS[:, 0:n], in0=pw, scalar1=HMC[:], scalar2=MAGIC,
            op0=ALU.mult, op1=ALU.add), inc="ve")
        sch.emit("vector", lambda e: e.tensor_scalar(
            out=PREPS[:, 0:n], in0=PREPS[:, 0:n], scalar1=MAGIC, scalar2=QMAX,
            op0=ALU.subtract, op1=ALU.min), inc="ve")
        if fold_scale:
            sch.emit("vector", lambda e: e.tensor_scalar(
                out=PREPS[:, 0:n], in0=PREPS[:, 0:n], scalar1=-QMAX - 1.0,
                scalar2=None, op0=ALU.max), inc="ve")
            sch.emit("vector", lambda e: e.tensor_scalar(
                out=PREPR[:], in0=PREPR[:], scalar1=1.0 / QMAX, scalar2=None,
                op0=ALU.mult), inc="ve", small=True)
            return sch.emit("vector", lambda e, pb=pb: e.tensor_scalar(
                out=pb, in0=PREPS[:, 0:n], scalar1=PREPR[:],
                scalar2=None, op0=ALU.mult),
                waits=[("pe", prepb_war.get(par, 0))], inc="ve")
        return sch.emit("vector", lambda e, pb=pb: e.tensor_scalar(
            out=pb, in0=PREPS[:, 0:n], scalar1=-QMAX - 1.0,
            scalar2=None, op0=ALU.max),
            waits=[("pe", prepb_war.get(par, 0))], inc="ve")

    prepb_war = {}
    prepo_war = {}

    def prep_transpose_store(nblocks, dst_fn, tq, par=0, extra_src_waits=()):
        pbank = par * 2
        for g0 in range(0, nblocks, 8):
            nb = min(8, nblocks - g0)
            tpe = None
            for j in range(nb):
                b = g0 + j
                tpe = transpose(
                    PSB[:, pbank * 1024 + j * 128:pbank * 1024 + (j + 1) * 128],
                    PREPB[:, par * 2816 + b * 128:par * 2816 + (b + 1) * 128],
                    banks=[pbank, pbank + 1],
                    waits=[("ve", tq), ("si", t_eye), ("sgi", t_eye_g)]
                    + list(extra_src_waits))
            prepb_war[par] = tpe
            tev = ps_read("scalar", lambda e, nb=nb, pk=pbank: e.activation(
                out=PREPO[:, par * 2816:par * 2816 + nb * 128],
                in_=PSB[:, pk * 1024:pk * 1024 + nb * 128], func=AF.Copy),
                banks=[pbank, pbank + 1],
                waits=[("sq_st%d" % par, prepo_war.get(par, 0))])
            dst, want_3d = dst_fn(g0, nb)
            po_slice = PREPO[:, par * 2816:par * 2816 + nb * 128]
            src = (po_slice.rearrange("p (i k) -> p i k", i=nb)
                   if want_3d else po_slice)
            prepo_war[par] = sch.dma("scalar", dst, src, "sq_st%d" % par,
                                     waits=[("se", tev)])

    def prep_all():
        kq = 0
        prepw_war = {}
        for which, wsrc, rs in ((0, w_gate, RSG), (1, w_up, RSU)):
            for ic in range(N_IC):
                par = kq % 2
                kq += 1
                tw = sch.dma("sync", PREPW[:, par * 2816:par * 2816 + 1024],
                             wsrc[ic * 128:(ic + 1) * 128, :],
                             "swq%d" % par,
                             waits=[("ve", prepw_war.get(par, 0))])
                sch.emit("vector", lambda e: e.engine_nop(),
                         waits=[("swq%d" % par, tw)], inc="ve")
                tq = quantize_rows(1024, fold_scale=False, rs_dst=rs,
                                   rs_col=ic, par=par)
                prepw_war[par] = tq

                def dstf(g0, nb, ic=ic, which=which):
                    return (wgu_s[:, ic * 2048 + which * 1024:
                                  ic * 2048 + which * 1024 + 1024], False)
                prep_transpose_store(N_CC, dstf, tq, par=par)
        for ot in range(8):
            par = kq % 2
            kq += 1
            tw = sch.dma("sync", PREPW[:, par * 2816:par * 2816 + 2816],
                         w_down[ot * 128:(ot + 1) * 128, :], "swq%d" % par,
                         waits=[("ve", prepw_war.get(par, 0))])
            sch.emit("vector", lambda e: e.engine_nop(),
                     waits=[("swq%d" % par, tw)], inc="ve")
            tq = quantize_rows(2816, fold_scale=True, par=par)
            prepw_war[par] = tq

            def dstf(g0, nb, ot=ot):
                return (wd_s.rearrange("p (i o) -> p i o", i=N_IC)[
                    :, g0:g0 + nb, ot * 128:(ot + 1) * 128], True)
            prep_transpose_store(N_IC, dstf, tq, par=par)
        for et in range(24 + 8):
            par = kq % 2
            kq += 1
            if et < 24:
                src = w_attn[et * 128:(et + 1) * 128, :]
            else:
                src = w_proj[(et - 24) * 128:(et - 24 + 1) * 128, :]
            tw = sch.dma("gpsimd", PREPB[:, par * 2816:par * 2816 + 1024],
                         src, "sg_ld%d" % par,
                         waits=[("pe", prepb_war.get(par, 0))])

            def dstf(g0, nb, et=et):
                if et < 16:
                    return (wqk_s[:, et * 1024:(et + 1) * 1024], False)
                if et < 24:
                    vt = et - 16
                    return (wv_s.rearrange("p (c v k) -> p c v k",
                                           c=N_CC, v=8)[:, :, vt], True)
                ot = et - 24
                return (wp_s.rearrange("p (c o k) -> p c o k",
                                       c=N_CC, o=8)[:, :, ot], True)
            prep_transpose_store(N_CC, dstf, tq=sch.val("ve"), par=par,
                                 extra_src_waits=[("sg_ld%d" % par, tw)])

    prep_all()
    dump("wgu", wgu_s, [128, N_IC * 16 * 128], BF16)
    dump("wd", wd_s, [128, N_IC * D], BF16)
    dump("wqk", wqk_s, [128, 16 * D], BF16)
    dump("wv", wv_s, [128, N_CC * D], BF16)
    dump("wp", wp_s, [128, N_CC * D], BF16)
    dump("rsg", RSG[:], [128, N_IC], F32)
    dump("rsu", RSU[:], [128, N_IC], F32)

    if stop_after == "prep":
        return _finish(nc, sch, dbg_outs)

    def ffn(pos):
        if pos == 1:
            phase_barrier()
            ln_stats(lambda t: (XRES[:, t * 1024:(t + 1) * 1024], []),
                     with_max=True)
            tred = ln_finalize(with_max=True)
            tgm = partition_max_to_scalar(RMAX[:], tred)
            tb = global_max(0, tgm)
        else:
            tb = tcc1
        sch.emit("vector", lambda e: e.tensor_tensor(
            out=SCL1[0:1, 1:2], in0=SIGX[0:1, 0:1], in1=GMAXS[0:1, 0:1],
            op=ALU.mult), waits=[("sgc", tb), ("ve", t_sig)],
            inc="ve", small=True)
        sch.emit("vector", lambda e: e.tensor_scalar(
            out=SCL1[0:1, 1:2], in0=SCL1[0:1, 1:2], scalar1=1.0 / QMAX,
            scalar2=None, op0=ALU.mult), inc="ve", small=True)
        t3 = sch.emit("vector", lambda e: e.reciprocal(
            out=SCL1[0:1, 0:1], in_=SCL1[0:1, 1:2]), inc="ve", small=True)
        tbc = bcast_scalars(0, 2, [("ve", t3)])
        sch.emit("vector", lambda e: e.tensor_scalar(
            out=QA[:], in0=RSTD[:], scalar1=SCL[:, 0:1], scalar2=None,
            op0=ALU.mult), waits=[("se", tbc)], inc="ve", small=True)
        sch.emit("vector", lambda e: e.tensor_tensor(
            out=QB[:], in0=MEAN[:], in1=QA[:], op=ALU.mult),
            inc="ve", small=True)
        tqb = sch.emit("vector", lambda e: e.tensor_scalar(
            out=QB[:], in0=QB[:], scalar1=-1.0, scalar2=None, op0=ALU.mult),
            inc="ve", small=True)
        sch.emit("vector", lambda e: e.tensor_scalar(
            out=SGATE[:], in0=RSG[:], scalar1=SCL[:, 1:2], scalar2=None,
            op0=ALU.mult), waits=[("se", tbc)], inc="ve", small=True)
        tsg = sch.emit("vector", lambda e: e.tensor_scalar(
            out=SUP[:], in0=RSU[:], scalar1=SCL[:, 1:2], scalar2=None,
            op0=ALU.mult), inc="ve", small=True)
        tmx = sch.emit("vector", lambda e: e.memset(RMAX[:], 0.0),
                       inc="ve", small=True)

        phase_barrier()
        gev_war = {}
        hev_war = {}
        hsp_war = {}
        wgu_tok = {}
        xint_pe = {}
        xqtm_pe = 0
        n_st = p3_st if p3_st is not None else N_FFN_ST
        n_ic = p3_ic if p3_ic is not None else N_IC
        for st in range(n_st):
            spar = st % 2
            xoff = spar * N_CC * FFN_ST
            t_xint_se = None
            for tt in range(FFN_ST // 128):
                t = st * (FFN_ST // 128) + tt
                if pos == 0:
                    buf = tt % 2
                    xr = XTILE[:, buf * 1024:(buf + 1) * 1024]
                    sem = "sx%d" % buf
                    tload = sch.dma("sync", xr, x_in[t * 128:(t + 1) * 128, :],
                                    sem, waits=[("ve", sch.val("ve"))])
                    wsrc = [(sem, tload)]
                else:
                    xr = XRES[:, t * 1024:(t + 1) * 1024]
                    wsrc = []
                sch.emit("vector", lambda e, xr=xr, t=t: e.tensor_scalar(
                    out=QTMP[:], in0=xr, scalar1=QA[:, t:t + 1],
                    scalar2=QB[:, t:t + 1], op0=ALU.mult, op1=ALU.add),
                    waits=wsrc + [("ve", tqb), ("se", sch.val("se"))], inc="ve")
                sch.emit("vector", lambda e: e.tensor_scalar(
                    out=QTMP[:], in0=QTMP[:], scalar1=MAGIC, scalar2=MAGIC,
                    op0=ALU.add, op1=ALU.subtract), inc="ve")
                tq = sch.emit("vector", lambda e: e.tensor_scalar(
                    out=XQTM[:], in0=QTMP[:], scalar1=QMAX, scalar2=-QMAX - 1.0,
                    op0=ALU.min, op1=ALU.max),
                    waits=[("pe", xqtm_pe)], inc="ve")
                bank = 6 + (tt % 2)
                for b in range(N_CC):
                    xqtm_pe = transpose(
                        PSB[:, bank * 1024 + b * 128:bank * 1024 + (b + 1) * 128],
                        XQTM[:, b * 128:(b + 1) * 128], banks=[bank],
                        waits=[("ve", tq), ("sgi", t_eye_g)])
                dst = XINT[:, xoff:xoff + N_CC * FFN_ST].rearrange(
                    "p (c k) -> p c k", c=N_CC)[:, :, tt * 128:(tt + 1) * 128]
                srcp = PSB[:, bank * 1024:(bank + 1) * 1024].rearrange(
                    "p (c k) -> p c k", c=N_CC)
                t_xint_se = ps_read("scalar",
                                    lambda e, d=dst, s=srcp: e.activation(
                                        out=d, in_=s, func=AF.Copy),
                                    banks=[bank],
                                    waits=[("pe", xint_pe.get(spar, 0))])
            for ic in range(n_ic):
                wbuf = ic % 2
                wsem = "sw%d" % wbuf
                tw = sch.dma("sync", WGUB[:, wbuf * 2048:(wbuf + 1) * 2048],
                             wgu_s[:, ic * 2048:(ic + 1) * 2048], wsem,
                             waits=[("pe", wgu_tok.get(wbuf, 0))])
                par = ic % 2
                gbanks = [0, 1] if par == 0 else [2, 3]
                ubanks = [4, 5] if par == 0 else [6, 7]
                mm_w = [(wsem, tw), ("se", t_xint_se)]
                tok = None
                for base, banks in ((0, gbanks), (1024, ubanks)):
                    for c in range(N_CC):
                        lhs = WGUB[:, wbuf * 2048 + base + c * 128:
                                   wbuf * 2048 + base + (c + 1) * 128]
                        for half in range(2):
                            rhs = XINT[:, xoff + c * FFN_ST + half * 512:
                                       xoff + c * FFN_ST + half * 512 + 512]
                            tok = mm(PS[:, banks[half] * 512:
                                        banks[half] * 512 + 512],
                                     lhs, rhs, banks=[banks[half]],
                                     start=(c == 0), stop=(c == N_CC - 1),
                                     waits=mm_w)
                wgu_tok[wbuf] = tok
                xint_pe[spar] = tok
                gbuf = ic % 2
                hbuf = ic % 2
                gslc = GEV[:, gbuf * 1024:(gbuf + 1) * 1024]
                hslc = HEV[:, hbuf * 1024:(hbuf + 1) * 1024]
                tsilu = ps_read("scalar",
                                lambda e, g=gslc, ic=ic, b=gbanks: e.activation(
                                    out=g,
                                    in_=PS[:, b[0] * 512:b[0] * 512 + 1024],
                                    func=AF.Silu, scale=SGATE[:, ic:ic + 1]),
                                banks=gbanks,
                                waits=[("ve", tsg),
                                       ("ve", gev_war.get(gbuf, 0))])
                tstt = ps_read("vector",
                               lambda e, g=gslc, h=hslc, ic=ic, b=ubanks:
                                   e.scalar_tensor_tensor(
                                       out=h,
                                       in0=PS[:, b[0] * 512:b[0] * 512 + 1024],
                                       scalar=SUP[:, ic:ic + 1], in1=g,
                                       op0=ALU.mult, op1=ALU.mult),
                               banks=ubanks,
                               waits=[("se", tsilu), ("ve", tmx),
                                      ("po", hev_war.get(hbuf, 0))])
                gev_war[gbuf] = tstt
                sch.emit("vector", lambda e, h=hslc: e.tensor_reduce(
                    out=HMC[:], in_=h, axis=AX, op=ALU.max,
                    apply_absolute_value=True), inc="ve", small=True)
                sch.emit("vector", lambda e: e.tensor_tensor(
                    out=RMAX[:], in0=RMAX[:], in1=HMC[:], op=ALU.max),
                    inc="ve", small=True)
                tcast = sch.emit("gpsimd",
                                 lambda e, h=hslc, b=hbuf: e.tensor_copy(
                                     HSP[:, b * 1024:(b + 1) * 1024], h),
                                 waits=[("ve", tstt),
                                        ("sq%d" % hbuf, hsp_war.get(hbuf, 0))],
                                 inc="po")
                hev_war[hbuf] = tcast
                tspill = sch.dma("scalar",
                                 h_s[:, ic * T_CORE + st * FFN_ST:
                                     ic * T_CORE + (st + 1) * FFN_ST],
                                 HSP[:, hbuf * 1024:(hbuf + 1) * 1024],
                                 "sq%d" % hbuf, waits=[("po", tcast)])
                hsp_war[hbuf] = tspill

        if pos == 0:
            dump("h1", h_s, [128, N_IC * T_CORE], BF16)
            dump("s1a", SCL[:, 0:4], [128, 4], F32)
            dump("qa", QA[:], [128, N_TILES], F32)
            dump("qb", QB[:], [128, N_TILES], F32)
        if stop_after == ("ffn%d_p3" % (pos + 1)):
            return None

        tgm2 = partition_max_to_scalar(RMAX[:], sch.val("ve"))
        tb2 = global_max(1, tgm2)
        sch.emit("vector", lambda e: e.tensor_tensor(
            out=SCL1[0:1, 3:4], in0=SIGX[0:1, 1:2], in1=GMAXS[0:1, 1:2],
            op=ALU.mult), waits=[("sgc", tb2)], inc="ve", small=True)
        sch.emit("vector", lambda e: e.tensor_scalar(
            out=SCL1[0:1, 3:4], in0=SCL1[0:1, 3:4], scalar1=1.0 / QMAX,
            scalar2=None, op0=ALU.mult), inc="ve", small=True)
        sch.emit("vector", lambda e: e.reciprocal(
            out=SCL1[0:1, 2:3], in_=SCL1[0:1, 3:4]), inc="ve", small=True)
        t5 = sch.emit("vector", lambda e: e.tensor_scalar(
            out=SCL1[0:1, 3:4], in0=SCL1[0:1, 3:4], scalar1=0.5, scalar2=None,
            op0=ALU.mult), inc="ve", small=True)
        tbc2 = bcast_scalars(2, 2, [("ve", t5)])
        if pos == 0:
            dump("s2a", SCL[:, 0:4], [128, 4], F32)

        phase_barrier()
        twd = sch.dma("sync", WDT[:], wd_s, "swd")
        hq_pe = {}
        hraw_war = {}
        res_tok = {}
        out_toks = []
        h_s3 = h_s.rearrange("p (i t) -> p i t", i=N_IC)
        tq_ic_st = {}

        def p5_quantize(st):
            spar = st % 2
            hoff = spar * N_IC * P5_ST
            tq_ic = {}
            for ic0 in range(0, N_IC, 2):
                nic = min(2, N_IC - ic0)
                w = nic * P5_ST
                buf = (ic0 // 2) % 2
                tl = sch.dma("gpsimd",
                             HRAW[:, buf * 512:buf * 512 + w],
                             h_s3[:, ic0:ic0 + nic,
                                  st * P5_ST:(st + 1) * P5_ST],
                             "sg%d" % buf,
                             waits=[("ve", hraw_war.get(buf, 0))])
                sch.emit("vector", lambda e, b=buf, w=w: e.tensor_scalar(
                    out=P5S[:, 0:w], in0=HRAW[:, b * 512:b * 512 + w],
                    scalar1=SCL[:, 2:3], scalar2=MAGIC, op0=ALU.mult,
                    op1=ALU.add),
                    waits=[("sg%d" % buf, tl), ("se", tbc2),
                           ("pe", hq_pe.get(spar, 0))],
                    inc="ve")
                sch.emit("vector", lambda e, w=w: e.tensor_scalar(
                    out=P5S[:, 0:w], in0=P5S[:, 0:w], scalar1=MAGIC,
                    scalar2=QMAX, op0=ALU.subtract, op1=ALU.min), inc="ve")
                tq = sch.emit("vector", lambda e, ic0=ic0, ho=hoff, w=w:
                              e.tensor_scalar(
                    out=HQ[:, ho + ic0 * P5_ST:ho + ic0 * P5_ST + w],
                    in0=P5S[:, 0:w],
                    scalar1=-QMAX - 1.0, scalar2=None, op0=ALU.max), inc="ve")
                for k in range(nic):
                    tq_ic[ic0 + k] = tq
                hraw_war[buf] = tq
            tq_ic_st[st] = tq_ic

        p5_quantize(0)
        for st in range(N_P5_ST):
            spar = st % 2
            hoff = spar * N_IC * P5_ST
            tq_ic = tq_ic_st.pop(st)
            for tb_i in range(P5_ST // 128):
                t = st * (P5_ST // 128) + tb_i
                pair = t % 2
                banks = [0, 1] if pair == 0 else [2, 3]
                if pos == 0:
                    rsem = "sr%d" % (t % 2)
                    tload = sch.dma("sync", XRES[:, t * 1024:(t + 1) * 1024],
                                    x_in[t * 128:(t + 1) * 128, :], rsem,
                                    waits=[("ve", res_tok.get(t % 2, 0))])
                    res_w = [(rsem, tload)]
                else:
                    res_w = []
                tok = None
                for ic in range(N_IC):
                    lhs = HQ[:, hoff + ic * P5_ST + tb_i * 128:
                             hoff + ic * P5_ST + (tb_i + 1) * 128]
                    for half in range(2):
                        rhs = WDT[:, ic * D + half * 512:
                                  ic * D + half * 512 + 512]
                        tok = mm(PS[:, banks[half] * 512:banks[half] * 512 + 512],
                                 lhs, rhs, banks=[banks[half]],
                                 start=(ic == 0), stop=(ic == N_IC - 1),
                                 waits=[("ve", tq_ic[ic]), ("swd", twd)])
                hq_pe[spar] = tok
                if tb_i == 0 and st + 1 < N_P5_ST:
                    p5_quantize(st + 1)
                xslc = XRES[:, t * 1024:(t + 1) * 1024]
                tev = ps_read("vector",
                              lambda e, xs=xslc, b=banks:
                                  e.scalar_tensor_tensor(
                                      out=xs,
                                      in0=PS[:, b[0] * 512:b[0] * 512 + 1024],
                                      scalar=SCL[:, 3:4], in1=xs,
                                      op0=ALU.mult, op1=ALU.add),
                              banks=banks, waits=res_w + [("se", tbc2)])
                if pos == 0:
                    res_tok[t % 2] = tev
                if pos == 1:
                    tst = sch.dma("scalar", out[t * 128:(t + 1) * 128, :],
                                  xslc, "so", waits=[("ve", tev)])
                    out_toks.append(tst)
        return out_toks

    ffn(0)
    dump("x1", XRES[:], [128, N_TILES * 1024], F32)
    if stop_after in ("ffn1_p3", "ffn1"):
        return _finish(nc, sch, dbg_outs)

    def attention():
        phase_barrier()
        ln_stats(lambda t: (XRES[:, t * 1024:(t + 1) * 1024], []),
                 with_max=False)
        ln_finalize(with_max=False)
        sch.emit("vector", lambda e: e.tensor_tensor(
            out=QB[:], in0=MEAN[:], in1=RSTD[:], op=ALU.mult),
            waits=[("se", sch.val("se"))], inc="ve", small=True)
        tnm = sch.emit("vector", lambda e: e.tensor_scalar(
            out=QB[:], in0=QB[:], scalar1=-1.0, scalar2=None, op0=ALU.mult),
            inc="ve", small=True)

        ysb_pe = 0
        xlnf_pe = 0
        xln_pe = 0
        wqk_tok = {}
        wv_tok = {}
        wp_tok = {}
        norm_toks = {}
        n_ast = att_st if att_st is not None else N_ATT_ST

        def emit_ty(r, qt, g, tnrm, ebuf):
            nonlocal ysb_pe
            wloc = qt // 2
            for h8 in range(8):
                for jc in range(2):
                    off = h8 * 256 + jc * 128
                    transpose(
                        PSB[:, 4 * 1024 + off:4 * 1024 + off + 128],
                        EXPS[:, ebuf + off:ebuf + off + 128],
                        banks=[4 + off // 1024],
                        waits=[("ve", tnrm), ("sgi", t_eye_g)])
            tevt = ps_read("scalar", lambda e: e.activation(
                out=ATTT[:], in_=PSB[:, 4096:6144], func=AF.Copy),
                banks=[4, 5])
            tp = (0, 64) if g else None
            for h8 in range(8):
                h = 2 * h8 + g
                for jc in range(2):
                    jb = wloc * 2 + jc
                    lhs = VS[:, jb * D + h * 64:jb * D + (h + 1) * 64]
                    rhs = ATTT[:, h8 * 256 + jc * 128:
                               h8 * 256 + (jc + 1) * 128]
                    oslc = PS[:, 3072 + h8 * 128:3072 + (h8 + 1) * 128]
                    oslc = (oslc[64:128, :] if g else oslc[0:64, :])
                    mm(oslc, lhs, rhs, banks=[6 + h8 // 4],
                       start=(jc == 0), stop=(jc == 1),
                       waits=[("se", tevt)], tile_position=tp)
            dsty = YSB.rearrange("p (pr k) -> p pr k", pr=NH // 2)[
                g * 64:g * 64 + 64, :, qt * 128:(qt + 1) * 128]
            srcy = PS[g * 64:g * 64 + 64, 3072:4096].rearrange(
                "p (pr k) -> p pr k", pr=8)
            ps_read("scalar", lambda e, d=dsty, s=srcy: e.activation(
                out=d, in_=s, func=AF.Copy),
                banks=[6, 7], waits=[("pe", ysb_pe)])

        t_x_se_par = {}

        def emit_ln(st2):
            nonlocal xln_pe, xlnf_pe
            lpar = st2 % 2
            loff = lpar * N_CC * ATT_ST
            t_x_se = None
            for tt in range(ATT_ST // 128):
                t = st2 * (ATT_ST // 128) + tt
                xs = XRES[:, t * 1024:(t + 1) * 1024]
                tln = sch.emit("vector", lambda e, xs=xs, t=t: e.tensor_scalar(
                    out=XLN[:], in0=xs, scalar1=RSTD[:, t:t + 1],
                    scalar2=QB[:, t:t + 1], op0=ALU.mult, op1=ALU.add),
                    waits=[("ve", tnm), ("se", sch.val("se")),
                           ("pe", xln_pe)], inc="ve")
                bank = 6 + (tt % 2)
                for b in range(N_CC):
                    xln_pe = transpose(
                        PSB[:, bank * 1024 + b * 128:bank * 1024 + (b + 1) * 128],
                        XLN[:, b * 128:(b + 1) * 128], banks=[bank],
                        waits=[("ve", tln), ("sgi", t_eye_g)])
                dst = XLNF[:, loff:loff + N_CC * ATT_ST].rearrange(
                    "p (c k) -> p c k", c=N_CC)[:, :, tt * 128:(tt + 1) * 128]
                srcp = PSB[:, bank * 1024:(bank + 1) * 1024].rearrange(
                    "p (c k) -> p c k", c=N_CC)
                t_x_se = ps_read("scalar",
                                 lambda e, d=dst, s=srcp: e.activation(
                                     out=d, in_=s, func=AF.Copy),
                                 banks=[bank], waits=[("pe", xlnf_pe)])
            t_x_se_par[lpar] = t_x_se

        emit_ln(0)
        for st in range(n_ast):
            lpar = st % 2
            loff = lpar * N_CC * ATT_ST
            XLNF_ST = XLNF[:, loff:loff + N_CC * ATT_ST]
            t_x_se = t_x_se_par[lpar]
            for grp4 in range(4):
                for j in range(4):
                    qkc = grp4 * 4 + j
                    wbuf = qkc % 2
                    wsem = "sw%d" % wbuf
                    tw = sch.dma("sync", WQKB[:, wbuf * 1024:(wbuf + 1) * 1024],
                                 wqk_s[:, qkc * 1024:(qkc + 1) * 1024], wsem,
                                 waits=[("pe", wqk_tok.get(wbuf, 0))])
                    tok = None
                    for c in range(N_CC):
                        lhs = WQKB[:, wbuf * 1024 + c * 128:
                                   wbuf * 1024 + (c + 1) * 128]
                        rhs = XLNF_ST[:, c * ATT_ST:(c + 1) * ATT_ST]
                        tok = mm(PS[:, j * 512:j * 512 + 512], lhs, rhs,
                                 banks=[j], start=(c == 0),
                                 stop=(c == N_CC - 1),
                                 waits=[(wsem, tw), ("se", t_x_se)])
                    wqk_tok[wbuf] = tok
                    xlnf_pe = max(xlnf_pe, tok)
                scale = 0.125 if grp4 < 2 else 1.0
                ps_read("scalar", lambda e, g4=grp4, s=scale: e.activation(
                    out=QKS[:, g4 * 2048:(g4 + 1) * 2048],
                    in_=PS[:, 0:2048], func=AF.Copy, scale=np.float32(s)),
                    banks=[0, 1, 2, 3])
            for c in range(N_CC):
                wbuf = c % 2
                vsem = "sv%d" % wbuf
                tw = sch.dma("sync", WVB[:, wbuf * 1024:(wbuf + 1) * 1024],
                             wv_s[:, c * D:(c + 1) * D], vsem,
                             waits=[("pe", wv_tok.get(wbuf, 0))])
                tok = None
                for tb_i in range(4):
                    lhs = XLNF_ST[:, c * ATT_ST + tb_i * 128:
                                  c * ATT_ST + (tb_i + 1) * 128]
                    for half in range(2):
                        bank = tb_i * 2 + half
                        rhs = WVB[:, wbuf * 1024 + half * 512:
                                  wbuf * 1024 + (half + 1) * 512]
                        tok = mm(PS[:, bank * 512:bank * 512 + 512], lhs, rhs,
                                 banks=[bank], start=(c == 0),
                                 stop=(c == N_CC - 1),
                                 waits=[(vsem, tw), ("se", t_x_se)])
                wv_tok[wbuf] = tok
            for tb_i in range(4):
                ps_read("scalar", lambda e, tb_i=tb_i: e.activation(
                    out=VS[:, tb_i * D:(tb_i + 1) * D],
                    in_=PS[:, tb_i * 1024:(tb_i + 1) * 1024], func=AF.Copy),
                    banks=[tb_i * 2, tb_i * 2 + 1])
            if st + 1 < n_ast:
                emit_ln(st + 1)
            rounds = [(qt, g) for qt in range(4) for g in range(2)]
            prev = None
            for r0, (qt, g) in enumerate(rounds):
                r = st * 8 + r0
                wloc = qt // 2
                ebuf = (r % 2) * 2048
                tnrm = None
                for hf in range(2):
                    for h8 in range(hf * 4, hf * 4 + 4):
                        qc = h8
                        lhs = QKS[:, qc * ATT_ST + qt * 128:
                                  qc * ATT_ST + (qt + 1) * 128][
                            g * 64:g * 64 + 64, :]
                        rhs = QKS[:, (8 + qc) * ATT_ST + wloc * WIN:
                                  (8 + qc) * ATT_ST + (wloc + 1) * WIN][
                            g * 64:g * 64 + 64, :]
                        mm(PS[:, h8 * 256:h8 * 256 + 256], lhs, rhs,
                           banks=[h8 // 2], start=True, stop=True)
                    hb = [2 * hf, 2 * hf + 1]
                    eoff = ebuf + hf * 1024
                    texp = ps_read("scalar", lambda e, eo=eoff, hf=hf:
                                   e.activation(
                        out=EXPS[:, eo:eo + 1024],
                        in_=PS[:, hf * 1024:hf * 1024 + 1024], func=AF.Exp,
                        bias=np.float32(0.0), scale=np.float32(1.0)),
                        banks=hb,
                        waits=[("ve", norm_toks.get((r - 2, hf), 0))])
                    sch.emit("vector", lambda e, eo=eoff, hf=hf: e.tensor_reduce(
                        out=SUMS[:, hf * 4:hf * 4 + 4],
                        in_=EXPS[:, eo:eo + 1024].rearrange(
                            "p (h j) -> p h j", h=4), axis=AX, op=ALU.add),
                        waits=[("se", texp)], inc="ve", small=True)
                    trec = sch.emit("vector", lambda e, hf=hf: e.reciprocal(
                        out=RECS[:, hf * 4:hf * 4 + 4],
                        in_=SUMS[:, hf * 4:hf * 4 + 4]), inc="ve", small=True)
                    tnrm = sch.emit("vector", lambda e, eo=eoff, hf=hf:
                                    e.tensor_tensor(
                        out=EXPS[:, eo:eo + 1024].rearrange(
                            "p (h j) -> p h j", h=4),
                        in0=EXPS[:, eo:eo + 1024].rearrange(
                            "p (h j) -> p h j", h=4),
                        in1=RECS[:, hf * 4:hf * 4 + 4].rearrange(
                            "p (h o) -> p h o", o=1)
                            .broadcast_to([128, 4, WIN]),
                        op=ALU.mult), waits=[("ve", trec)], inc="ve")
                    norm_toks[(r, hf)] = tnrm
                if prev is not None:
                    emit_ty(*prev)
                prev = (r, qt, g, tnrm, ebuf)
            emit_ty(*prev)
            t_y_se = sch.val("se")
            tokp = None
            for p in range(NH // 2):
                wbuf = p % 2
                psem = "sv%d" % wbuf
                tw = sch.dma("sync", WPB[:, wbuf * 1024:(wbuf + 1) * 1024],
                             wp_s[:, p * D:(p + 1) * D], psem,
                             waits=[("pe", wv_tok.get(wbuf, 0))])
                for tb_i in range(4):
                    lhs = YSB[:, p * ATT_ST + tb_i * 128:
                              p * ATT_ST + (tb_i + 1) * 128]
                    for half in range(2):
                        bank = tb_i * 2 + half
                        tokp = mm(PS[:, bank * 512:bank * 512 + 512], lhs,
                                  WPB[:, wbuf * 1024 + half * 512:
                                      wbuf * 1024 + (half + 1) * 512],
                                  banks=[bank], start=(p == 0),
                                  stop=(p == NH // 2 - 1),
                                  waits=[(psem, tw), ("se", t_y_se)])
                wv_tok[wbuf] = tokp
            ysb_pe = tokp
            for tb_i in range(4):
                t = st * 4 + tb_i
                banks = [tb_i * 2, tb_i * 2 + 1]
                xslc = XRES[:, t * 1024:(t + 1) * 1024]
                ps_read("vector", lambda e, xs=xslc, b=banks: e.tensor_tensor(
                    out=xs, in0=PS[:, b[0] * 512:b[0] * 512 + 1024], in1=xs,
                    op=ALU.add), banks=banks)

    attention()
    dump("x2", XRES[:], [128, N_TILES * 1024], F32)
    if stop_after == "attn":
        return _finish(nc, sch, dbg_outs)

    ffn(1)
    return _finish(nc, sch, dbg_outs)


def _finish(nc, sch, dbg_outs):
    final = sch.val("so")
    if final > 0:
        sch.emit("sync", lambda e: e.nop(), waits=[("so", final)])
    sch.build()
    return nc, dbg_outs


def _in_maps(inputs):
    x = np.ascontiguousarray(
        np.asarray(inputs["L0_attn_x"], dtype=np.float32).reshape(-1, D))
    shards = [x[r * T_CORE:(r + 1) * T_CORE] for r in range(N_CORES)]
    eye_f32 = np.eye(128, dtype=np.float32)
    base = dict(
        w_attn=np.ascontiguousarray(np.asarray(inputs["L0_attn_w_attn"], np.float32)),
        w_proj=np.ascontiguousarray(np.asarray(inputs["L0_attn_w_proj"], np.float32)),
        w_gate=np.ascontiguousarray(np.asarray(inputs["L0_ffn_w_gate"], np.float32)),
        w_up=np.ascontiguousarray(np.asarray(inputs["L0_ffn_w_up"], np.float32)),
        w_down=np.ascontiguousarray(np.asarray(inputs["L0_ffn_w_down"], np.float32)),
        clip_x=np.asarray(inputs["L0_ffn_clip_x"], np.float32).reshape(1, 1),
        clip_down=np.asarray(inputs["L0_ffn_clip_down"], np.float32).reshape(1, 1),
        eye_f32=eye_f32,
    )
    return [dict(base, x=shards[r]) for r in range(N_CORES)]


def kernel(**inputs):
    nc, _ = build_program()
    in_maps = _in_maps(inputs)
    res = run_bass_kernel_spmd(nc, in_maps, core_ids=list(range(N_CORES)))
    parts = [res.results[r]["out"] for r in range(N_CORES)]
    return np.concatenate(parts, axis=0).reshape(4, 8192, D).astype(np.float32)
